# version 16
# baseline (speedup 1.0000x reference)
import math
from contextlib import ExitStack

import numpy as np
import ml_dtypes

import concourse.bass as bass
import concourse.mybir as mybir
from concourse.bass_utils import run_bass_kernel_spmd

F32 = mybir.dt.float32
BF16 = mybir.dt.bfloat16
I32 = mybir.dt.int32
AF = mybir.ActivationFunctionType
ALU = mybir.AluOpType
AX = mybir.AxisListType

B, S, D = 4, 2048, 4096
NH, HD = 32, 128
NT = 16
ATT_SCALE = 1.0 / math.sqrt(HD)
DEBUG = False

def _Lj(j):
    hi = 12 if j < 4 else 16
    return list(range(0, j + 1)) + list(range(8, hi))

_SC_KBS = [sorted(set().union(*[set(_Lj(j)) for j in rng]))
           for rng in (range(0, 4), range(4, 8))]
_PAIRS = [(j, kb) for j in range(8) for kb in _Lj(j)]
NPAIR = len(_PAIRS)
_PAIR_IDX = {p: i for i, p in enumerate(_PAIRS)}


def _alibi_slopes(n_heads):
    cp2 = 2 ** math.floor(math.log2(n_heads))
    base = 2.0 ** (-2.0 ** (-(math.log2(cp2) - 3)))
    slopes = [base ** i for i in range(1, cp2 + 1)]
    if cp2 != n_heads:
        eb = 2.0 ** (-2.0 ** (-(math.log2(2 * cp2) - 3)))
        ne = min(cp2, n_heads - cp2)
        slopes += [eb ** i for i in range(1, 2 * ne + 1, 2)]
    return np.asarray(slopes, dtype=np.float64)


def own_chrono_blocks(half):
    return [0, 1, 2, 3, 12, 13, 14, 15] if half == 0 else [4, 5, 6, 7, 8, 9, 10, 11]


def perm_chrono_blocks(half):
    own = own_chrono_blocks(half)
    return own + [t for t in range(NT) if t not in own]


WIDE_H0 = 8
WKB = {0: _SC_KBS[0], 1: _SC_KBS[1]}
BAND = {0: [0, 1, 2, 3], 1: [4, 5, 6, 7]}
NWF = len(WKB[0]) + len(WKB[1])


def _chrono_q_range(half, r):
    perm = perm_chrono_blocks(half)
    return [perm[j] for j in range(4 * r, 4 * r + 4)]


def build_fwide(half):
    perm = perm_chrono_blocks(half)
    slopes = _alibi_slopes(NH)
    out = np.zeros((128, (NH - WIDE_H0) * NWF), dtype=np.float64)
    p = np.arange(128, dtype=np.float64)
    for h in range(WIDE_H0, NH):
        sl = slopes[h]
        for r in (0, 1):
            cqs = _chrono_q_range(half, r)
            cr = min(cqs) * 128 + 255.5
            for i, kb in enumerate(WKB[r]):
                ck = perm[kb]
                col = (h - WIDE_H0) * NWF + (0 if r == 0 else len(WKB[0])) + i
                if kb in BAND[r]:
                    out[:, col] = np.exp(sl * (ck * 128 + p - cr))
                elif ck <= min(cqs):
                    out[:, col] = np.exp(sl * (ck * 128 + p - cr))
    return out.astype(np.float32)


def build_maskwide(half):
    perm = perm_chrono_blocks(half)
    out = np.zeros((128, 8 * 512), dtype=np.float32)
    kp = np.arange(128)[:, None]
    for r in (0, 1):
        for i, kb in enumerate(BAND[r]):
            ck = perm[kb]
            t = r * 4 + i
            for jj in range(4):
                cq = perm[4 * r + jj]
                qf = np.arange(128)[None, :]
                valid = (ck * 128 + kp) <= (cq * 128 + qf)
                out[:, t * 512 + jj * 128: t * 512 + (jj + 1) * 128] = valid
    return out


_KEEP_CACHE = {}


def _keepers():
    if _KEEP_CACHE:
        return _KEEP_CACHE
    fn = {half: build_fcols(half) for half in (0, 1)}
    fw = {half: build_fwide(half) for half in (0, 1)}
    kn = {}
    for h in range(8):
        for (j, kb), i in _PAIR_IDX.items():
            kn[(h, j, kb)] = bool(fn[0][:, h * NPAIR + i].any() or fn[1][:, h * NPAIR + i].any())
    kw = {}
    for h in range(WIDE_H0, NH):
        for r in (0, 1):
            for i, kb in enumerate(WKB[r]):
                ci = (h - WIDE_H0) * NWF + (0 if r == 0 else len(WKB[0])) + i
                kw[(h, r, kb)] = bool(fw[0][:, ci].any() or fw[1][:, ci].any())
    _KEEP_CACHE.update(dict(kn=kn, kw=kw))
    return _KEEP_CACHE


def build_fcols(half):
    perm = perm_chrono_blocks(half)
    slopes = _alibi_slopes(NH)
    out = np.zeros((128, 8 * NPAIR), dtype=np.float64)
    p = np.arange(128, dtype=np.float64)
    for h in range(8):
        sl = slopes[h]
        for i, (j, kb) in enumerate(_PAIRS):
            cq, ck = perm[j], perm[kb]
            if ck > cq:
                continue
            out[:, h * NPAIR + i] = np.exp(sl * (ck * 128 + p - (cq * 128 + 63.5)))
    return out.astype(np.float32)


class Emitter:
    ENGINES = ("sync", "gpsimd", "vector", "scalar", "tensor")

    def __init__(self, nc, sems):
        self.nc = nc
        self.sems = sems
        self.ops = {e: [] for e in self.ENGINES}
        self.cnt = {k: 0 for k in sems}

    DMA_SEMS = frozenset({"ld0", "ld1", "ld2", "ldm", "lax0", "lax1", "lax2", "lax3", "wtc",
                          "wp0", "wp1", "wr0", "wr1", "wr2", "wr3", "scr0", "scr1", "scr2",
                          "scr3", "so0", "so1", "so2", "so3", "dbg"})

    phase = "init"

    def op(self, eng, fn, wait=(), inc=None, inc_n=None):
        if inc is not None and inc_n is None:
            inc_n = 16 if inc in self.DMA_SEMS else 1
        sems = self.sems
        nc = self.nc
        wait = [(s, v) for (s, v) in wait if v > 0]
        def run(e, fn=fn, wait=tuple(wait), inc=inc, inc_n=inc_n, ph=self.phase):
            with nc.named_scope(ph):
                for s, v in wait:
                    e.wait_ge(sems[s], v)
                ins = fn(e)
                if inc is not None:
                    ins.then_inc(sems[inc], inc_n)
        self.ops[eng].append(run)
        if inc is not None:
            self.cnt[inc] += inc_n
            return self.cnt[inc]
        return None

    def reserve(self, sem, n=1):
        self.cnt[sem] += n
        return self.cnt[sem]


def build_nc(debug=DEBUG):
    nc = bass.Bass("TRN2", target_bir_lowering=False)

    x_d = nc.dram_tensor("x", [S, D], F32, kind="ExternalInput").ap()
    wq_d = nc.dram_tensor("wq", [D, D], F32, kind="ExternalInput").ap()
    wkv_d = nc.dram_tensor("wkv", [D, 2 * HD], F32, kind="ExternalInput").ap()
    wo_d = nc.dram_tensor("wo", [D, D], F32, kind="ExternalInput").ap()
    wqi_d = nc.dram_tensor("wqi", [D, D], I32, kind="ExternalInput").ap()
    scales_d = nc.dram_tensor("scales", [D, D // 32], F32, kind="ExternalInput").ap()
    ident_d = nc.dram_tensor("ident", [128, 128], F32, kind="ExternalInput").ap()
    dmask_d = nc.dram_tensor("dmask", [128, 128], F32, kind="ExternalInput").ap()
    fcols_d = nc.dram_tensor("fcols", [128, 8 * NPAIR], F32, kind="ExternalInput").ap()
    fwide_d = nc.dram_tensor("fwide", [128, (NH - 8) * NWF], F32, kind="ExternalInput").ap()
    mwide_d = nc.dram_tensor("mwide", [128, 8 * 512], F32, kind="ExternalInput").ap()

    out_d = nc.dram_tensor("out", [1024, D], F32, kind="ExternalOutput").ap()
    scr_d = nc.dram_tensor("scr_awo", [1024, D], BF16, kind="Internal").ap()

    dbg = {}
    if debug:
        for nm, shape in [("dbg_xnt", [128, 1024]), ("dbg_kt", [128, 2048]),
                          ("dbg_vt", [128, NT * 129]), ("dbg_qt", [128, 2048]),
                          ("dbg_at", [128, 2048])]:
            dbg[nm] = nc.dram_tensor(nm, shape, F32, kind="ExternalOutput").ap()

    with ExitStack() as stk:
        en = stk.enter_context
        sems = {k: en(nc.semaphore(k)) for k in (
            "ld0", "ld1", "ld2", "ldm", "lax0", "lax1", "lax2", "lax3", "wtc", "wp0", "wp1",
            "wr0", "wr1", "wr2", "wr3", "scr0", "scr1", "scr2", "scr3", "so0", "so1", "so2", "so3", "dbg",
            "pe", "ve", "se", "ge")}
        E = Emitter(nc, sems)

        xnt = en(nc.sbuf_tensor("xnt", [128, 32 * 1024], BF16))
        kt = en(nc.sbuf_tensor("kt", [128, NT * 128], BF16))
        vt = en(nc.sbuf_tensor("vt", [128, NT * 129], BF16))
        ident = en(nc.sbuf_tensor("ident_s", [128, 128], BF16))
        dmask = en(nc.sbuf_tensor("dmask_s", [128, 128], BF16))

        stA = ExitStack()
        enA = stA.enter_context
        xblk = enA(nc.sbuf_tensor("xblk", [128, 3 * D], F32))
        xn_tm = enA(nc.sbuf_tensor("xn_tm", [128, 2 * D], BF16))
        xnt_o = enA(nc.sbuf_tensor("xnt_o", [128, 4 * D], BF16))
        wkvb = enA(nc.sbuf_tensor("wkvb", [128, 32 * 256], BF16))
        lnst = enA(nc.sbuf_tensor("lnst", [128, 2 * 48], F32))
        lnag = enA(nc.sbuf_tensor("lnag", [128, 2 * 2], F32))
        lnmv = enA(nc.sbuf_tensor("lnmv", [128, 2 * 4], F32))
        psA_tr = enA(nc.psum_tensor("psA_tr", [128, 2048], BF16))
        psA_k = enA(nc.psum_tensor("psA_k", [128, 512], F32))
        psA_v = enA(nc.psum_tensor("psA_v", [128, 512], F32))

        E.phase = "A"
        ge_ones = E.op("gpsimd", lambda e: e.memset(
            vt[:].rearrange("p (t c) -> p t c", c=129)[:, :, 128:129], 1.0), inc="ge")

        E.op("gpsimd", lambda e: e.dma_start(out=ident[:], in_=ident_d), inc="wtc")
        E.op("gpsimd", lambda e: e.dma_start(out=dmask[:], in_=dmask_d), inc="wtc")
        wt_consts = E.op("gpsimd", lambda e: e.dma_start(
            out=wkvb[:].rearrange("p (t c) -> p t c", c=256),
            in_=wkv_d.rearrange("(t p) c -> p t c", p=128)), inc="wtc")

        ve_xn = [0] * NT
        pe_tpr = [[0] * 4 for _ in range(NT)]
        veA_evr = [[0] * 4 for _ in range(NT)]
        seA_evr = [[0] * 4 for _ in range(NT)]
        pe_v = [0] * NT
        se_evv = [0] * NT
        pe_k = [0] * 4
        se_evk = [0] * 4
        ld_x = [0] * NT

        for t in range(NT):
            b3 = t % 3
            b2 = t % 2
            ld_x[t] = E.op("sync", lambda e, b3=b3, t=t: e.dma_start(
                out=xblk[:, b3 * D:(b3 + 1) * D], in_=x_d[t * 128:(t + 1) * 128, :]),
                wait=[("ve", ve_xn[t - 3] if t >= 3 else 0)], inc=f"ld{t % 3}")

            c1 = E.reserve("ve")
            c2 = E.reserve("ve")
            def lnstats(e, b2=b2, b3=b3, c1=c1, c2=c2):
                for c in range(8):
                    ins = e.bn_stats(lnst[:, b2 * 48 + c * 6: b2 * 48 + (c + 1) * 6],
                                     xblk[:, b3 * D + c * 512: b3 * D + (c + 1) * 512])
                ins.then_inc(sems["ve"], 1)
                e.wait_ge(sems["ve"], c1)
                e.bn_aggr(lnag[:, b2 * 2:(b2 + 1) * 2], lnst[:, b2 * 48:(b2 + 1) * 48]).then_inc(sems["ve"], 1)
                e.wait_ge(sems["ve"], c2)
                e.tensor_copy(lnmv[:, b2 * 4: b2 * 4 + 1], lnag[:, b2 * 2: b2 * 2 + 1])
                return e.tensor_scalar(lnmv[:, b2 * 4 + 1: b2 * 4 + 2],
                                       lnag[:, b2 * 2 + 1: b2 * 2 + 2], 1e-5, None, ALU.add)
            ve_stats = E.op("vector", lnstats,
                wait=[(f"ld{t % 3}", ld_x[t]), ("ve", ve_xn[t - 2] if t >= 2 else 0)], inc="ve")
            se_sq = E.op("scalar", lambda e, b2=b2: e.activation(
                lnmv[:, b2 * 4 + 2: b2 * 4 + 3], lnmv[:, b2 * 4 + 1: b2 * 4 + 2], AF.Sqrt),
                wait=[("ve", ve_stats)], inc="se")
            c3 = E.reserve("ve")
            def lnapply(e, b2=b2, b3=b3, c3=c3):
                e.reciprocal(lnmv[:, b2 * 4 + 3: b2 * 4 + 4],
                             lnmv[:, b2 * 4 + 2: b2 * 4 + 3]).then_inc(sems["ve"], 1)
                e.wait_ge(sems["ve"], c3)
                return e.tensor_scalar(xn_tm[:, b2 * D:(b2 + 1) * D], xblk[:, b3 * D:(b3 + 1) * D],
                                       lnmv[:, b2 * 4: b2 * 4 + 1], lnmv[:, b2 * 4 + 3: b2 * 4 + 4],
                                       ALU.subtract, ALU.mult)
            ve_xn[t] = E.op("vector", lnapply,
                wait=[("se", se_sq), ("pe", pe_tpr[t - 2][3] if t >= 2 else 0)], inc="ve")

            for g in range(4):
                gid = t * 4 + g
                ph_ = gid % 2
                if gid >= 2:
                    pg, pgg = (gid - 2) // 4, (gid - 2) % 4
                    prev_ev = seA_evr[pg][pgg]
                    prev_sem = "se" if (gid - 2) % 2 == 0 else "ve"
                else:
                    prev_ev, prev_sem = 0, "se"
                def tpr(e, b2=b2, g=g, ph_=ph_):
                    ins = None
                    for i in range(8):
                        dt = g * 8 + i
                        ins = e.transpose(psA_tr[:, ph_ * 1024 + i * 128: ph_ * 1024 + (i + 1) * 128],
                                          xn_tm[:, b2 * D + dt * 128: b2 * D + (dt + 1) * 128],
                                          ident[:])
                    return ins
                waits = [("ve", ve_xn[t]), (prev_sem, prev_ev)]
                if t == 0 and g == 0:
                    waits.append(("wtc", 48))
                pe_tpr[t][g] = E.op("tensor", tpr, wait=waits, inc="pe")
                if t < 8:
                    def evr(e, t=t, g=g, ph_=ph_):
                        dst = bass.AP(xnt, (g * 8) * 1024 + t * 128, [[32 * 1024, 128], [1024, 8], [1, 128]])
                        src_ = psA_tr[:, ph_ * 1024:(ph_ + 1) * 1024].rearrange("p (i c) -> p i c", c=128)
                        return e.copy(dst, src_) if hasattr(e, "copy") else e.tensor_copy(dst, src_)
                else:
                    def evr(e, t=t, g=g, ph_=ph_):
                        dst = bass.AP(xnt_o, (g * 8) * 512 + (t % 4) * 128, [[4 * D, 128], [512, 8], [1, 128]])
                        src_ = psA_tr[:, ph_ * 1024:(ph_ + 1) * 1024].rearrange("p (i c) -> p i c", c=128)
                        return e.copy(dst, src_) if hasattr(e, "copy") else e.tensor_copy(dst, src_)
                eng = "scalar" if gid % 2 == 0 else "vector"
                seA_evr[t][g] = E.op(eng, evr, wait=[("pe", pe_tpr[t][g])],
                                     inc="se" if gid % 2 == 0 else "ve")

            def vmm(e, t=t):
                ins = None
                for dt in range(32):
                    if t < 8:
                        lhsT = xnt[:, dt * 1024 + t * 128: dt * 1024 + (t + 1) * 128]
                    else:
                        lhsT = xnt_o[:, dt * 512 + (t % 4) * 128: dt * 512 + (t % 4 + 1) * 128]
                    ins = e.matmul(psA_v[:, 0:128], lhsT, wkvb[:, dt * 256 + 128: (dt + 1) * 256],
                                   start=(dt == 0), stop=(dt == 31))
                return ins
            pe_v[t] = E.op("tensor", vmm,
                           wait=[("se" if (t * 4 + 3) % 2 == 0 else "ve", seA_evr[t][3]),
                                 ("se" if (t * 4 + 2) % 2 == 0 else "ve", seA_evr[t][2]),
                                 ("wtc", 48),
                                 ("se", se_evv[t - 1] if t >= 1 else 0)], inc="pe")
            se_evv[t] = E.op("scalar", lambda e, t=t: e.copy(vt[:, t * 129: t * 129 + 128], psA_v[:, 0:128]),
                             wait=[("pe", pe_v[t])], inc="se")

            if t % 4 == 3:
                c = t // 4
                def kmm(e, c=c):
                    ins = None
                    for dt in range(32):
                        if c < 2:
                            rhs = xnt[:, dt * 1024 + c * 512: dt * 1024 + (c + 1) * 512]
                        else:
                            rhs = xnt_o[:, dt * 512: (dt + 1) * 512]
                        ins = e.matmul(psA_k[:, 0:512], wkvb[:, dt * 256: dt * 256 + 128], rhs,
                                       start=(dt == 0), stop=(dt == 31))
                    return ins
                pe_k[c] = E.op("tensor", kmm,
                               wait=[("se", se_evk[c - 1] if c >= 1 else 0)], inc="pe")
                se_evk[c] = E.op("scalar", lambda e, c=c: e.copy(kt[:, c * 512:(c + 1) * 512], psA_k[:, 0:512]),
                                 wait=[("pe", pe_k[c])], inc="se")

        A_ve, A_se, A_pe = E.cnt["ve"], E.cnt["se"], E.cnt["pe"]
        A_DONE = [("pe", A_pe), ("ve", A_ve), ("se", A_se)]
        stA.close()

        if debug:
            E.op("gpsimd", lambda e: e.dma_start(out=dbg["dbg_kt"], in_=kt[:]),
                 wait=[("se", A_se)], inc="dbg")
            E.op("gpsimd", lambda e: e.dma_start(out=dbg["dbg_vt"], in_=vt[:]),
                 wait=[("se", A_se), ("ge", ge_ones)], inc="dbg")
            E.op("gpsimd", lambda e: e.dma_start(
                out=dbg["dbg_xnt"], in_=xnt[:, 0 * 1024: 1 * 1024]),
                wait=[("se", A_se)], inc="dbg")

        stB = ExitStack()
        enB = stB.enter_context
        qat = enB(nc.sbuf_tensor("qat", [128, NH * 1024], BF16))
        wpan = enB(nc.sbuf_tensor("wpan", [128, 2 * 32 * 256], BF16))
        fcols = enB(nc.sbuf_tensor("fcols_s", [128, 8 * NPAIR], F32))
        fwide = enB(nc.sbuf_tensor("fwide_s", [128, (NH - 8) * NWF], F32))
        mwide = enB(nc.sbuf_tensor("mwide_s", [128, 8 * 512], BF16))
        expb = enB(nc.sbuf_tensor("expb", [128, 3072], BF16))
        attn_tm = enB(nc.sbuf_tensor("attn_tm", [128, 2 * 512], BF16))
        zz = enB(nc.sbuf_tensor("zz", [128, 16], F32))
        attn_un = enB(nc.sbuf_tensor("attn_un", [128, 1024], BF16))
        vf = enB(nc.sbuf_tensor("vf", [128, 2 * 16 * 129], BF16))
        awo = enB(nc.sbuf_tensor("awo", [128, 4 * 256], BF16))

        ld_fc = E.op("sync", lambda e: e.dma_start(out=fcols[:], in_=fcols_d), wait=A_DONE, inc="ldm")
        E.op("sync", lambda e: e.dma_start(out=fwide[:], in_=fwide_d), inc="ldm")
        wt_mw = E.op("gpsimd", lambda e: e.dma_start(out=mwide[:], in_=mwide_d), wait=A_DONE, inc="wtc")
        ld_fc = 32

        stPB = ExitStack()
        psB = stPB.enter_context(nc.psum_tensor("psB", [128, 2048], F32))

        E.phase = "B"
        wt_pan = [0] * 16
        pe_pang = {}
        se_evq = {}
        group_i = 0
        for p in range(16):
            bufp = p % 2
            warw = [("pe", pe_pang[(p - 2, 3)])] if p >= 2 else list(A_DONE)
            wt_pan[p] = E.op("gpsimd", lambda e, p=p, bufp=bufp: e.dma_start(
                out=wpan[:, bufp * 8192: (bufp + 1) * 8192].rearrange("p (t c) -> p t c", c=256),
                in_=wq_d[:, p * 256:(p + 1) * 256].rearrange("(t q) c -> q t c", q=128)),
                wait=warw, inc=f"wp{p % 2}")
            for sub in range(2):
                for qc in range(2):
                    g = sub * 2 + qc
                    quarter = group_i % 4
                    group_i += 1
                    def bmm(e, bufp=bufp, sub=sub, qc=qc, quarter=quarter):
                        ins = None
                        for dt in range(32):
                            ins = e.matmul(
                                psB[:, quarter * 512:(quarter + 1) * 512],
                                wpan[:, bufp * 8192 + dt * 256 + sub * 128: bufp * 8192 + dt * 256 + (sub + 1) * 128],
                                xnt[:, dt * 1024 + qc * 512: dt * 1024 + (qc + 1) * 512],
                                start=(dt == 0), stop=(dt == 31))
                        return ins
                    waits = [(f"wp{p % 2}", wt_pan[p]), ("se", se_evq.get(group_i - 1 - 4, 0))]
                    if p == 0 and g == 0:
                        waits.append(("se", A_se))
                        waits.append(("pe", A_pe))
                    pe_pang[(p, g)] = E.op("tensor", bmm, wait=waits, inc="pe")
                    h = p * 2 + sub
                    se_evq[group_i - 1] = E.op("scalar", lambda e, h=h, qc=qc, quarter=quarter: e.mul(
                        qat[:, h * 1024 + qc * 512: h * 1024 + (qc + 1) * 512],
                        psB[:, quarter * 512:(quarter + 1) * 512], ATT_SCALE),
                        wait=[("pe", pe_pang[(p, g)])], inc="se")
        B_se = E.cnt["se"]

        dbg_qt_cnt = 0
        if debug:
            dbg_qt_cnt = E.op("gpsimd", lambda e: e.dma_start(out=dbg["dbg_qt"], in_=qat[:, 0:2048]),
                              wait=[("se", B_se)], inc="dbg")

        E.phase = "C"
        stPB.close()
        stPC = ExitStack()
        psC_sc = stPC.enter_context(nc.psum_tensor("psC_sc", [128, 2048], F32))
        psC_av = stPC.enter_context(nc.psum_tensor("psC_av", [128, 1024], F32))
        psC_tr = stPC.enter_context(nc.psum_tensor("psC_tr", [128, 1024], BF16))

        AVOFF = [0, 129, 512, 641]
        se_exp = {}
        ve_f = {}
        ge_f = {}
        pe_avg = {}
        se_evA = {}
        ve_evB = {}
        pe_ctp = {}
        se_evat = {}
        pending = None

        def _flush_tail(p):
            i = p["idx"]
            zr_p = E.reserve("ve")
            fn_p = p["evB"]
            ve_evB[i] = E.op("vector", lambda e, fn_p=fn_p, zr_p=zr_p: fn_p(e, zr=zr_p),
                             wait=[("ve", se_evA[i]), ("pe", pe_ctp.get(i - 2, 0))], inc="ve")
            pe_ctp[i] = E.op("tensor", p["ctp"],
                             wait=[("ve", ve_evB[i]), ("ve", se_evat.get(i - 2, 0))], inc="pe")
            w = [("pe", pe_ctp[i])]
            if p["dbg"]:
                w.append(("dbg", dbg_qt_cnt))
            se_evat[i] = E.op("vector", p["evat"], wait=w, inc="ve")

        it_c = 0
        KP = _keepers()
        last_av = {}
        fst = {"v": True, "g": True}
        for r in range(2):
            for h in range(NH):
                if h >= WIDE_H0:
                    kbs = [kb for kb in _SC_KBS[r] if KP["kw"][(h, r, kb)]]
                else:
                    kbs = [kb for kb in _SC_KBS[r]
                           if any(kb in set(_Lj(4 * r + jj)) and KP["kn"][(h, 4 * r + jj, kb)]
                                  for jj in range(4))]
                groups = [kbs[i:i + 2] for i in range(0, len(kbs), 2)]
                G = len(groups)
                emits = []
                for gi2, grp2 in enumerate(groups):
                    for i2, kb2 in enumerate(grp2):
                        for jj2 in range(4):
                            j2 = 4 * r + jj2
                            if kb2 not in set(_Lj(j2)):
                                continue
                            if h < WIDE_H0 and not KP["kn"][(h, j2, kb2)]:
                                continue
                            emits.append((gi2, i2, kb2, jj2))
                bank_first = {}
                bank_last = {}
                for em in emits:
                    b = em[3] // 2
                    bank_first.setdefault(b, em)
                    bank_last[b] = em
                assert set(bank_first) == {0, 1}, (r, h, kbs)
                emits = set(emits)
                idx0 = r * NH + h
                bufv = idx0 % 2
                ve_vf = 0
                vfb_todo = None
                if h >= WIDE_H0:
                    def vfb(e, h=h, r=r, kbs=kbs, bufv=bufv):
                        ins = None
                        for ki, kb in enumerate(kbs):
                            ci = ((h - WIDE_H0) * NWF
                                  + (0 if r == 0 else len(WKB[0])) + WKB[r].index(kb))
                            ins = e.tensor_scalar(
                                vf[:, bufv * 2064 + ki * 129: bufv * 2064 + (ki + 1) * 129],
                                vt[:, kb * 129: (kb + 1) * 129],
                                fwide[:, ci: ci + 1], None, ALU.mult)
                        return ins
                    vfw = [("ldm", 32), ("ge", ge_ones)]
                    if idx0 >= 2:
                        vfw.append(("pe", last_av[idx0 - 2]))
                    vfb_todo = (vfb, vfw)
                for gi in range(G + 1):
                    if gi < G:
                        grp = groups[gi]
                        slot = gi % 2
                        it = it_c + gi
                        eslot = it % 3
                        def smm(e, grp=grp, slot=slot, h=h, r=r):
                            ins = None
                            for i, kb in enumerate(grp):
                                ins = e.matmul(psC_sc[:, slot * 1024 + i * 512: slot * 1024 + (i + 1) * 512],
                                               kt[:, kb * 128:(kb + 1) * 128],
                                               qat[:, h * 1024 + r * 512: h * 1024 + (r + 1) * 512],
                                               start=True, stop=True)
                            return ins
                        waits = [("se", se_exp.get(it - 2, 0))]
                        if it == 0:
                            waits += [("se", B_se)]
                        sc_pe = E.op("tensor", smm, wait=waits, inc="pe")
                        if gi == min(3, G - 1) and pending is not None:
                            i_p = pending["idx"]
                            pe_ctp[i_p] = E.op("tensor", pending["ctp"],
                                               wait=[("ve", ve_evB[i_p]),
                                                     ("ve", se_evat.get(i_p - 2, 0))], inc="pe")
                            w = [("pe", pe_ctp[i_p])]
                            if pending["dbg"]:
                                w.append(("dbg", dbg_qt_cnt))
                            se_evat[i_p] = E.op("vector", pending["evat"], wait=w, inc="ve")
                            pending = None
                        se_exp[it] = E.op("scalar", lambda e, slot=slot, eslot=eslot, n=len(grp): e.activation(
                            expb[:, eslot * 1024: eslot * 1024 + n * 512],
                            psC_sc[:, slot * 1024: slot * 1024 + n * 512], AF.Exp),
                            wait=[("pe", sc_pe), ("pe", pe_avg.get(it - 3, 0))], inc="se")
                        def fm(e, grp=grp, slot=eslot, h=h, r=r, jjs=(0, 1, 2, 3)):
                            ins = None
                            for i, kb in enumerate(grp):
                                if h >= WIDE_H0:
                                    sl = expb[:, slot * 1024 + i * 512: slot * 1024 + (i + 1) * 512]
                                    if kb in BAND[r]:
                                        t = r * 4 + BAND[r].index(kb)
                                        ins = e.tensor_tensor(sl, sl,
                                                              mwide[:, t * 512:(t + 1) * 512],
                                                              ALU.mult)
                                    continue
                                for jj in jjs:
                                    j = 4 * r + jj
                                    if kb not in set(_Lj(j)) or not _KEEP_CACHE["kn"][(h, j, kb)]:
                                        continue
                                    col = slot * 1024 + i * 512 + jj * 128
                                    fc = fcols[:, h * NPAIR + _PAIR_IDX[(j, kb)]: h * NPAIR + _PAIR_IDX[(j, kb)] + 1]
                                    sl = expb[:, col: col + 128]
                                    if kb == j:
                                        ins = e.scalar_tensor_tensor(sl, sl, fc, dmask[:], ALU.mult, ALU.mult)
                                    else:
                                        ins = e.tensor_scalar(sl, sl, fc, None, ALU.mult)
                            return ins
                        fmw = [("se", se_exp[it])]
                        if fst["v"]:
                            fmw += [("ldm", ld_fc), ("wtc", wt_mw)]
                        if h >= WIDE_H0:
                            if any(kb in BAND[r] for kb in grp):
                                ve_f[it] = E.op("vector", fm, wait=fmw, inc="ve")
                                if fst["v"]:
                                    fst["v"] = False
                            else:
                                ve_f[it] = None
                            ge_f[it] = None
                        elif h < WIDE_H0:
                            gp_pairs = [(i, kb) for i, kb in enumerate(grp)
                                        if kb in set(_Lj(4 * r + 3)) and kb != 4 * r + 3
                                        and _KEEP_CACHE["kn"][(h, 4 * r + 3, kb)]]
                            vdiag = [(i, kb) for i, kb in enumerate(grp) if kb == 4 * r + 3]
                            def fmv(e, grp=grp, slot=eslot, h=h, r=r, vdiag=vdiag):
                                ins = fm(e, grp=grp, slot=slot, h=h, r=r, jjs=(0, 1, 2))
                                for i, kb in vdiag:
                                    j = 4 * r + 3
                                    if not _KEEP_CACHE["kn"][(h, j, kb)]:
                                        continue
                                    col = slot * 1024 + i * 512 + 3 * 128
                                    fc = fcols[:, h * NPAIR + _PAIR_IDX[(j, kb)]: h * NPAIR + _PAIR_IDX[(j, kb)] + 1]
                                    sl = expb[:, col: col + 128]
                                    ins = e.scalar_tensor_tensor(sl, sl, fc, dmask[:], ALU.mult, ALU.mult)
                                return ins
                            def fmg(e, grp=grp, slot=eslot, h=h, r=r, gp_pairs=gp_pairs):
                                ins = None
                                for i, kb in enumerate(grp):
                                    if (i, kb) not in gp_pairs:
                                        continue
                                    j = 4 * r + 3
                                    col = slot * 1024 + i * 512 + 3 * 128
                                    ci = h * NPAIR + _PAIR_IDX[(j, kb)]
                                    sl = expb[:, col: col + 128]
                                    ins = e.tensor_tensor(sl, sl,
                                                          fcols[:, ci: ci + 1].to_broadcast([128, 128]),
                                                          ALU.mult)
                                return ins
                            nv = sum(1 for i, kb in enumerate(grp) for jj in (0, 1, 2)
                                     if kb in set(_Lj(4 * r + jj)) and KP["kn"][(h, 4 * r + jj, kb)])
                            nv += sum(1 for i, kb in vdiag if KP["kn"][(h, 4 * r + 3, kb)])
                            gmw = [("se", se_exp[it])]
                            if fst["g"]:
                                gmw += [("ldm", ld_fc), ("wtc", wt_mw)]
                            if nv:
                                ve_f[it] = E.op("vector", fmv, wait=fmw, inc="ve")
                                if fst["v"]:
                                    fst["v"] = False
                            else:
                                ve_f[it] = None
                            if gp_pairs:
                                ge_f[it] = E.op("gpsimd", fmg, wait=gmw, inc="ge")
                                fst["g"] = False
                            else:
                                ge_f[it] = None
                        if gi == 0:
                            if pending is not None:
                                i_p = pending["idx"]
                                zr_p = E.reserve("ve")
                                fn_p = pending["evB"]
                                ve_evB[i_p] = E.op("vector",
                                                   lambda e, fn_p=fn_p, zr_p=zr_p: fn_p(e, zr=zr_p),
                                                   wait=[("ve", se_evA[i_p]),
                                                         ("pe", pe_ctp.get(i_p - 2, 0))], inc="ve")
                            if vfb_todo is not None:
                                ve_vf = E.op("vector", vfb_todo[0], wait=vfb_todo[1], inc="ve")
                                vfb_todo = None

                    if gi >= 1:
                        grp = groups[gi - 1]
                        it = it_c + gi - 1
                        eslot = it % 3
                        def avm(e, grp=grp, gidx=gi - 1, slot=eslot, r=r, h=h, kbs=kbs, bufv=bufv,
                                emits=emits, bank_first=bank_first, bank_last=bank_last):
                            ins = None
                            for i, kb in enumerate(grp):
                                if h >= WIDE_H0:
                                    rhs = vf[:, bufv * 2064 + kbs.index(kb) * 129:
                                             bufv * 2064 + (kbs.index(kb) + 1) * 129]
                                else:
                                    rhs = vt[:, kb * 129: (kb + 1) * 129]
                                for jj in range(4):
                                    key = (gidx, i, kb, jj)
                                    if key not in emits:
                                        continue
                                    b = jj // 2
                                    ins = e.matmul(psC_av[:, AVOFF[jj]: AVOFF[jj] + 129],
                                                   expb[:, slot * 1024 + i * 512 + jj * 128: slot * 1024 + i * 512 + (jj + 1) * 128],
                                                   rhs,
                                                   start=(bank_first[b] == key),
                                                   stop=(bank_last[b] == key))
                            return ins
                        avw = []
                        if ve_f.get(it):
                            avw.append(("ve", ve_f[it]))
                        else:
                            avw.append(("se", se_exp[it]))
                        if ge_f.get(it):
                            avw.append(("ge", ge_f[it]))
                        if gi == 1 and h >= WIDE_H0:
                            avw.append(("ve", ve_vf))
                        if gi == 1:
                            avw.append(("ve", se_evA.get(r * NH + h - 1, 0)))
                            if it == 0:
                                avw.append(("ge", ge_ones))
                        pe_avg[it] = E.op("tensor", avm, wait=avw, inc="pe")
                it_c += G
                last_av[idx0] = pe_avg[it_c - 1]

                av_fin = pe_avg[it_c - 1]
                idx = r * NH + h
                bufa = idx % 2
                half = idx % 2
                def evA(e, bufa=bufa):
                    src4 = bass.AP(psC_av, 0, [[1024, 128], [512, 2], [129, 2], [1, 128]])
                    srcz = bass.AP(psC_av, 128, [[1024, 128], [512, 2], [129, 2], [1, 1]])
                    e.tensor_copy(zz[:, bufa * 8: bufa * 8 + 4].rearrange("p (a b y) -> p a b y", a=2, b=2),
                                  srcz)
                    return e.tensor_copy(
                        attn_un[:, bufa * 512: (bufa + 1) * 512].rearrange("p (a b y) -> p a b y", a=2, b=2),
                        src4)
                se_evA[idx] = E.op("vector", evA,
                                   wait=[("pe", av_fin), ("ve", ve_evB.get(idx - 2, 0))], inc="ve")

                def evB(e, bufa=bufa, zr=None):
                    e.reciprocal(zz[:, bufa * 8 + 4: bufa * 8 + 8],
                                 zz[:, bufa * 8: bufa * 8 + 4]).then_inc(sems["ve"], 1)
                    e.wait_ge(sems["ve"], zr)
                    srcu = attn_un[:, bufa * 512: (bufa + 1) * 512].rearrange(
                        "p (a b y) -> p a b y", a=2, b=2)
                    dst = attn_tm[:, bufa * 512: (bufa + 1) * 512].rearrange(
                        "p (a b y) -> p a b y", a=2, b=2)
                    return e.tensor_tensor(dst, srcu,
                                           zz[:, bufa * 8 + 4: bufa * 8 + 8]
                                           .rearrange("p (a b) -> p a b", a=2)
                                           .to_broadcast([128, 2, 2, 128]), ALU.mult)
                def ctp(e, bufa=bufa, half=half):
                    ins = None
                    for jj in range(4):
                        ins = e.transpose(psC_tr[:, half * 512 + jj * 128: half * 512 + (jj + 1) * 128],
                                          attn_tm[:, bufa * 512 + jj * 128: bufa * 512 + (jj + 1) * 128],
                                          ident[:])
                    return ins
                def evat(e, half=half, h=h, r=r):
                    src_ = psC_tr[:, half * 512:(half + 1) * 512]
                    dst_ = qat[:, h * 1024 + r * 512: h * 1024 + (r + 1) * 512]
                    return e.copy(dst_, src_) if hasattr(e, "copy") else e.tensor_copy(dst_, src_)
                pending = {"evB": evB, "ctp": ctp, "evat": evat, "idx": idx,
                           "dbg": bool(debug and r == 0 and h < 2)}


        if pending is not None:
            _flush_tail(pending)
            pending = None
        C_se = E.cnt["se"]
        C_ve = E.cnt["ve"]

        if debug:
            E.op("gpsimd", lambda e: e.dma_start(out=dbg["dbg_at"], in_=qat[:, 0:2048]),
                 wait=[("se", C_se), ("ve", C_ve)], inc="dbg")

        E.phase = "D"
        stPC.close()
        stPD = ExitStack()
        psB = stPD.enter_context(nc.psum_tensor("psD", [128, 2048], F32))
        it_d = 0
        pe_dmm = {}
        ve_evd = {}
        st_awo = {}
        wt_dpan = [0] * 16
        for p in range(16):
            bufp = p % 2
            war = pe_dmm[(p - 2, 7)] if p >= 2 else pe_pang[(14 + p, 3)]
            wt_dpan[p] = E.op("gpsimd", lambda e, p=p, bufp=bufp: e.dma_start(
                out=wpan[:, bufp * 8192: (bufp + 1) * 8192].rearrange("p (t c) -> p t c", c=256),
                in_=wo_d[:, p * 256:(p + 1) * 256].rearrange("(t q) c -> q t c", q=128)),
                wait=[("pe", war)], inc=f"wp{p % 2}")
            for qb in range(8):
                slot = it_d % 4
                def dmm(e, bufp=bufp, qb=qb, slot=slot):
                    ins = None
                    for h in range(NH):
                        ins = e.matmul(psB[:, slot * 512: slot * 512 + 256],
                                       qat[:, h * 1024 + qb * 128: h * 1024 + (qb + 1) * 128],
                                       wpan[:, bufp * 8192 + h * 256: bufp * 8192 + (h + 1) * 256],
                                       start=(h == 0), stop=(h == NH - 1))
                    return ins
                waits = [(f"wp{p % 2}", wt_dpan[p]), ("ve", ve_evd.get(it_d - 4, 0))]
                if it_d == 0:
                    waits.append(("se", C_se))
                    waits.append(("ve", C_ve))
                pe_dmm[(p, qb)] = E.op("tensor", dmm, wait=waits, inc="pe")
                def dev(e, slot=slot, it_d=it_d):
                    return e.tensor_copy(awo[:, slot * 256:(slot + 1) * 256],
                                         psB[:, slot * 512: slot * 512 + 256])
                ve_evd[it_d] = E.op("vector", dev,
                                    wait=[("pe", pe_dmm[(p, qb)]),
                                          (f"scr{it_d % 4}", st_awo.get(it_d - 4, 0))], inc="ve")
                st_awo[it_d] = E.op("sync", lambda e, p=p, qb=qb, slot=slot: e.dma_start(
                    out=scr_d[qb * 128:(qb + 1) * 128, p * 256:(p + 1) * 256],
                    in_=awo[:, slot * 256:(slot + 1) * 256]),
                    wait=[("ve", ve_evd[it_d])], inc=f"scr{it_d % 4}")
                it_d += 1
        D_pe = E.cnt["pe"]
        D_scr = [E.cnt[f"scr{i}"] for i in range(4)]
        D_DONE = ([("pe", D_pe), ("ve", E.cnt["ve"]), ("se", E.cnt["se"])]
                  + [(f"scr{i}", D_scr[i]) for i in range(4)])
        if debug:
            D_DONE.append(("dbg", E.cnt["dbg"]))
        stPD.close()
        stB.close()

        E.phase = "EE"
        stE = ExitStack()
        enE = stE.enter_context
        wqr = enE(nc.sbuf_tensor("wqr", [128, 4 * D], BF16))
        scalesb = enE(nc.sbuf_tensor("scalesb", [128, 32 * 128], F32))
        deq = enE(nc.sbuf_tensor("deq", [128, 4 * D], BF16))
        wtp = enE(nc.sbuf_tensor("wtp", [128, 2 * 32 * 256], BF16))
        awin = enE(nc.sbuf_tensor("awin", [128, 4 * 256], BF16))
        xin = enE(nc.sbuf_tensor("xin", [128, 4 * 256], F32))
        outst = enE(nc.sbuf_tensor("outst", [128, 4 * 256], F32))
        psE_tr = enE(nc.psum_tensor("psE_tr", [128, 2048], BF16))
        psE = enE(nc.psum_tensor("psE", [128, 2048], F32))

        ld_sc = E.op("sync", lambda e: e.dma_start(
            out=scalesb[:].rearrange("p (t c) -> p t c", c=128),
            in_=scales_d.rearrange("(t p) c -> p t c", p=128)), wait=D_DONE, inc="ldm")

        wt_wqr = [0] * 32
        ve_deq = [0] * 32
        pe_etp = [[0] * 4 for _ in range(32)]
        se_evw = [[0] * 4 for _ in range(32)]
        pe_emm = {}
        ve_cmb = {}
        st_out = {}
        ld_ax = {}
        it_e = 0

        def _e_prod_head(ob):
            b4 = ob % 4
            wqrw = ([("ve", ve_deq[ob - 4])] if ob >= 4 else list(D_DONE))
            wt_wqr[ob] = E.op("gpsimd", lambda e, ob=ob, b4=b4: e.dma_start(
                out=wqr[:, b4 * D:(b4 + 1) * D], in_=wqi_d[ob * 128:(ob + 1) * 128, :]),
                wait=wqrw, inc=f"wr{ob % 4}")
            def dq(e, ob=ob, b4=b4):
                return e.scalar_tensor_tensor(
                    deq[:, b4 * D:(b4 + 1) * D].rearrange("p (s c) -> p s c", c=32),
                    wqr[:, b4 * D:(b4 + 1) * D].rearrange("p (s c) -> p s c", c=32),
                    128.0,
                    scalesb[:, ob * 128:(ob + 1) * 128].to_broadcast([128, 128, 32]),
                    ALU.subtract, ALU.mult)
            dqw = [(f"wr{ob % 4}", wt_wqr[ob]), ("pe", pe_etp[ob - 4][3] if ob >= 4 else 0)]
            if ob == 0:
                dqw.append(("ldm", ld_sc))
            ve_deq[ob] = E.op("vector", dq, wait=dqw, inc="ve")

        def _e_prod_round(ob, g):
            b2 = ob % 4
            bufp = (ob // 2) % 2
            gid = ob * 4 + g
            ph = gid % 2
            if gid >= 2:
                pg, pgg = (gid - 2) // 4, (gid - 2) % 4
                prev = se_evw[pg][pgg]
                prev_sem = "se"
            else:
                prev, prev_sem = 0, "se"
            def etp(e, b2=b2, g=g, ph=ph):
                ins = None
                for i in range(8):
                    it = g * 8 + i
                    ins = e.transpose(psE_tr[:, ph * 1024 + i * 128: ph * 1024 + (i + 1) * 128],
                                      deq[:, b2 * D + it * 128: b2 * D + (it + 1) * 128],
                                      ident[:])
                return ins
            pe_etp[ob][g] = E.op("tensor", etp,
                                 wait=[("ve", ve_deq[ob]), (prev_sem, prev)], inc="pe")
            def evw(e, ob=ob, g=g, bufp=bufp, ph=ph):
                dst = bass.AP(wtp, bufp * 8192 + (g * 8) * 256 + (ob % 2) * 128,
                              [[2 * 32 * 256, 128], [256, 8], [1, 128]])
                src_ = psE_tr[:, ph * 1024:(ph + 1) * 1024].rearrange("p (i c) -> p i c", c=128)
                return e.copy(dst, src_) if hasattr(e, "copy") else e.tensor_copy(dst, src_)
            warw = pe_emm.get((ob // 2 - 2, 7), 0)
            se_evw[ob][g] = E.op("scalar", evw, wait=[("pe", pe_etp[ob][g]), ("pe", warw)],
                                 inc="se")

        pf_state = {"pf": 0}

        def _e_prefetch(it2):
            pp2, qb2 = it2 // 8, it2 % 8
            bio = it2 % 4
            axw = [("ve", ve_cmb.get(it2 - 4, 0))]
            if it2 == 0:
                axw += [(f"scr{i}", D_scr[i]) for i in range(4)]
            E.op("sync", lambda e, pp=pp2, qb=qb2, bio=bio: e.dma_start(
                out=awin[:, bio * 256:(bio + 1) * 256],
                in_=scr_d[qb * 128:(qb + 1) * 128, pp * 256:(pp + 1) * 256]),
                wait=axw, inc=f"lax{bio}")
            ld_ax[it2] = E.op("sync", lambda e, pp=pp2, qb=qb2, bio=bio: e.dma_start(
                out=xin[:, bio * 256:(bio + 1) * 256],
                in_=x_d[qb * 128:(qb + 1) * 128, pp * 256:(pp + 1) * 256]), inc=f"lax{bio}")

        def _e_consume(pp, qb):
            nonlocal it_e
            bufp = pp % 2
            slot = it_e % 4
            bio = it_e % 4
            while pf_state["pf"] <= min(it_e + 2, 127):
                _e_prefetch(pf_state["pf"])
                pf_state["pf"] += 1
            def emm(e, pp=pp, qb=qb, slot=slot, bufp=bufp):
                ins = None
                for dt in range(32):
                    ins = e.matmul(psE[:, slot * 512: slot * 512 + 256],
                                   xnt[:, dt * 1024 + qb * 128: dt * 1024 + (qb + 1) * 128],
                                   wtp[:, bufp * 8192 + dt * 256: bufp * 8192 + (dt + 1) * 256],
                                   start=(dt == 0), stop=(dt == 31))
                return ins
            obs = (2 * pp, 2 * pp + 1)
            pw_se = max(se_evw[o][g2] for o in obs for g2 in range(4))
            pe_emm[(pp, qb)] = E.op("tensor", emm,
                                    wait=[("se", pw_se),
                                          ("ve", ve_cmb.get(it_e - 4, 0))], inc="pe")
            e1 = E.reserve("ve")
            def cmb(e, slot=slot, bio=bio, e1=e1):
                e.tensor_tensor(outst[:, bio * 256:(bio + 1) * 256],
                                psE[:, slot * 512: slot * 512 + 256],
                                awin[:, bio * 256:(bio + 1) * 256],
                                ALU.add).then_inc(sems["ve"], 1)
                e.wait_ge(sems["ve"], e1)
                return e.tensor_tensor(outst[:, bio * 256:(bio + 1) * 256],
                                       outst[:, bio * 256:(bio + 1) * 256],
                                       xin[:, bio * 256:(bio + 1) * 256], ALU.add)
            ve_cmb[it_e] = E.op("vector", cmb,
                                wait=[("pe", pe_emm[(pp, qb)]), (f"lax{bio}", ld_ax[it_e]),
                                      (f"so{bio}", st_out.get(it_e - 4, 0))], inc="ve")
            st_out[it_e] = E.op("sync", lambda e, pp=pp, qb=qb, bio=bio: e.dma_start(
                out=out_d[qb * 128:(qb + 1) * 128, pp * 256:(pp + 1) * 256],
                in_=outst[:, bio * 256:(bio + 1) * 256]),
                wait=[("ve", ve_cmb[it_e])], inc=f"so{bio}")
            it_e += 1

        for ob in range(4):
            _e_prod_head(ob)
        for wave in range(17):
            obs = [2 * wave, 2 * wave + 1] if wave < 16 else []
            rounds = [(ob, g) for ob in obs for g in range(4)]
            for k in range(8):
                if wave >= 1:
                    _e_consume(wave - 1, k)
                if rounds:
                    _e_prod_round(*rounds[k])
                if k == 7 and wave < 15:
                    for ob2 in (2 * wave + 4, 2 * wave + 5):
                        if ob2 <= 31:
                            _e_prod_head(ob2)
        stE.close()

        finw = [(f"so{i}", E.cnt[f"so{i}"]) for i in range(4)]
        if debug:
            finw.append(("dbg", E.cnt["dbg"]))
        E.op("sync", lambda e: None, wait=finw)

        with nc.Block() as block:
            @block.sync
            def _(eng):
                for f in E.ops["sync"]:
                    f(eng)

            @block.gpsimd
            def _(eng):
                for f in E.ops["gpsimd"]:
                    f(eng)

            @block.vector
            def _(eng):
                for f in E.ops["vector"]:
                    f(eng)

            @block.scalar
            def _(eng):
                for f in E.ops["scalar"]:
                    f(eng)

            @block.tensor
            def _(eng):
                for f in E.ops["tensor"]:
                    f(eng)

    return nc


def _host_inputs(half, x_b, wq, wkv, wo, wqi, scales):
    own = own_chrono_blocks(half)
    perm = perm_chrono_blocks(half)
    idx = np.concatenate([np.arange(t * 128, (t + 1) * 128) for t in perm])
    xp = np.ascontiguousarray(x_b[idx])
    return {
        "x": xp, "wq": wq, "wkv": wkv, "wo": wo, "wqi": wqi, "scales": scales,
        "ident": np.eye(128, dtype=np.float32),
        "dmask": np.triu(np.ones((128, 128), dtype=np.float32)),
        "fcols": build_fcols(half),
        "fwide": build_fwide(half),
        "mwide": build_maskwide(half),
    }


def kernel(L0_attn_hidden_states, L0_attn_Wq, L0_attn_Wkv, L0_attn_Wo,
           L0_ffn_w_q, L0_ffn_scales):
    x = np.ascontiguousarray(np.asarray(L0_attn_hidden_states, dtype=np.float32))
    wq = np.ascontiguousarray(np.asarray(L0_attn_Wq, dtype=np.float32))
    wkv = np.ascontiguousarray(np.asarray(L0_attn_Wkv, dtype=np.float32))
    wo = np.ascontiguousarray(np.asarray(L0_attn_Wo, dtype=np.float32))
    wqi = np.ascontiguousarray(np.asarray(L0_ffn_w_q, dtype=np.int32))
    scales = np.ascontiguousarray(np.asarray(L0_ffn_scales, dtype=np.float32))

    nc = build_nc()
    in_maps = []
    for c in range(8):
        b, half = c // 2, c % 2
        in_maps.append(_host_inputs(half, x[b], wq, wkv, wo, wqi, scales))

    res = run_bass_kernel_spmd(nc, in_maps, core_ids=list(range(8)))

    out = np.empty((B, S, D), dtype=np.float32)
    for c in range(8):
        b, half = c // 2, c % 2
        oc = np.asarray(res.results[c]["out"], dtype=np.float32)
        for i, t in enumerate(own_chrono_blocks(half)):
            out[b, t * 128:(t + 1) * 128, :] = oc[i * 128:(i + 1) * 128, :]
    return out


# version 17
# speedup vs baseline: 1.2937x; 1.2937x over previous
import math
from contextlib import ExitStack

import numpy as np
import ml_dtypes

import concourse.bass as bass
import concourse.mybir as mybir
from concourse.bass_utils import run_bass_kernel_spmd

F32 = mybir.dt.float32
BF16 = mybir.dt.bfloat16
I32 = mybir.dt.int32
AF = mybir.ActivationFunctionType
ALU = mybir.AluOpType
AX = mybir.AxisListType

B, S, D = 4, 2048, 4096
NH, HD = 32, 128
NT = 16
ATT_SCALE = 1.0 / math.sqrt(HD)
DEBUG = False

def _Lj(j):
    hi = 12 if j < 4 else 16
    return list(range(0, j + 1)) + list(range(8, hi))

_SC_KBS = [sorted(set().union(*[set(_Lj(j)) for j in rng]))
           for rng in (range(0, 4), range(4, 8))]
_PAIRS = [(j, kb) for j in range(8) for kb in _Lj(j)]
NPAIR = len(_PAIRS)
_PAIR_IDX = {p: i for i, p in enumerate(_PAIRS)}


def _alibi_slopes(n_heads):
    cp2 = 2 ** math.floor(math.log2(n_heads))
    base = 2.0 ** (-2.0 ** (-(math.log2(cp2) - 3)))
    slopes = [base ** i for i in range(1, cp2 + 1)]
    if cp2 != n_heads:
        eb = 2.0 ** (-2.0 ** (-(math.log2(2 * cp2) - 3)))
        ne = min(cp2, n_heads - cp2)
        slopes += [eb ** i for i in range(1, 2 * ne + 1, 2)]
    return np.asarray(slopes, dtype=np.float64)


def own_chrono_blocks(half):
    return [0, 1, 2, 3, 12, 13, 14, 15] if half == 0 else [4, 5, 6, 7, 8, 9, 10, 11]


def perm_chrono_blocks(half):
    own = own_chrono_blocks(half)
    return own + [t for t in range(NT) if t not in own]


WIDE_H0 = 8
WKB = {0: _SC_KBS[0], 1: _SC_KBS[1]}
BAND = {0: [0, 1, 2, 3], 1: [4, 5, 6, 7]}
NWF = len(WKB[0]) + len(WKB[1])


def _chrono_q_range(half, r):
    perm = perm_chrono_blocks(half)
    return [perm[j] for j in range(4 * r, 4 * r + 4)]


def build_fwide(half):
    perm = perm_chrono_blocks(half)
    slopes = _alibi_slopes(NH)
    out = np.zeros((128, (NH - WIDE_H0) * NWF), dtype=np.float64)
    p = np.arange(128, dtype=np.float64)
    for h in range(WIDE_H0, NH):
        sl = slopes[h]
        for r in (0, 1):
            cqs = _chrono_q_range(half, r)
            cr = min(cqs) * 128 + 255.5
            for i, kb in enumerate(WKB[r]):
                ck = perm[kb]
                col = (h - WIDE_H0) * NWF + (0 if r == 0 else len(WKB[0])) + i
                if kb in BAND[r]:
                    out[:, col] = np.exp(sl * (ck * 128 + p - cr))
                elif ck <= min(cqs):
                    out[:, col] = np.exp(sl * (ck * 128 + p - cr))
    return out.astype(np.float32)


def build_maskwide(half):
    perm = perm_chrono_blocks(half)
    out = np.zeros((128, 8 * 512), dtype=np.float32)
    kp = np.arange(128)[:, None]
    for r in (0, 1):
        for i, kb in enumerate(BAND[r]):
            ck = perm[kb]
            t = r * 4 + i
            for jj in range(4):
                cq = perm[4 * r + jj]
                qf = np.arange(128)[None, :]
                valid = (ck * 128 + kp) <= (cq * 128 + qf)
                out[:, t * 512 + jj * 128: t * 512 + (jj + 1) * 128] = valid
    return out


_KEEP_CACHE = {}


def _keepers():
    if _KEEP_CACHE:
        return _KEEP_CACHE
    fn = {half: build_fcols(half) for half in (0, 1)}
    fw = {half: build_fwide(half) for half in (0, 1)}
    kn = {}
    for h in range(8):
        for (j, kb), i in _PAIR_IDX.items():
            kn[(h, j, kb)] = bool(fn[0][:, h * NPAIR + i].any() or fn[1][:, h * NPAIR + i].any())
    kw = {}
    for h in range(WIDE_H0, NH):
        for r in (0, 1):
            for i, kb in enumerate(WKB[r]):
                ci = (h - WIDE_H0) * NWF + (0 if r == 0 else len(WKB[0])) + i
                kw[(h, r, kb)] = bool(fw[0][:, ci].any() or fw[1][:, ci].any())
    _KEEP_CACHE.update(dict(kn=kn, kw=kw))
    return _KEEP_CACHE


def build_fcols(half):
    perm = perm_chrono_blocks(half)
    slopes = _alibi_slopes(NH)
    out = np.zeros((128, 8 * NPAIR), dtype=np.float64)
    p = np.arange(128, dtype=np.float64)
    for h in range(8):
        sl = slopes[h]
        for i, (j, kb) in enumerate(_PAIRS):
            cq, ck = perm[j], perm[kb]
            if ck > cq:
                continue
            out[:, h * NPAIR + i] = np.exp(sl * (ck * 128 + p - (cq * 128 + 63.5)))
    return out.astype(np.float32)


class Emitter:
    ENGINES = ("sync", "gpsimd", "vector", "scalar", "tensor")

    def __init__(self, nc, sems):
        self.nc = nc
        self.sems = sems
        self.ops = {e: [] for e in self.ENGINES}
        self.cnt = {k: 0 for k in sems}

    DMA_SEMS = frozenset({"ld0", "ld1", "ld2", "ldm", "lax0", "lax1", "lax2", "lax3", "wtc",
                          "wp0", "wp1", "wr0", "wr1", "wr2", "wr3", "scr0", "scr1", "scr2",
                          "scr3", "so0", "so1", "so2", "so3", "dbg"})

    phase = "init"

    def op(self, eng, fn, wait=(), inc=None, inc_n=None):
        if inc is not None and inc_n is None:
            inc_n = 16 if inc in self.DMA_SEMS else 1
        sems = self.sems
        nc = self.nc
        wait = [(s, v) for (s, v) in wait if v > 0]
        def run(e, fn=fn, wait=tuple(wait), inc=inc, inc_n=inc_n, ph=self.phase):
            with nc.named_scope(ph):
                for s, v in wait:
                    e.wait_ge(sems[s], v)
                ins = fn(e)
                if inc is not None:
                    ins.then_inc(sems[inc], inc_n)
        self.ops[eng].append(run)
        if inc is not None:
            self.cnt[inc] += inc_n
            return self.cnt[inc]
        return None

    def reserve(self, sem, n=1):
        self.cnt[sem] += n
        return self.cnt[sem]


def build_nc(debug=DEBUG):
    nc = bass.Bass("TRN2", target_bir_lowering=False)

    x_d = nc.dram_tensor("x", [S, D], F32, kind="ExternalInput").ap()
    wq_d = nc.dram_tensor("wq", [D, D], F32, kind="ExternalInput").ap()
    wkv_d = nc.dram_tensor("wkv", [D, 2 * HD], F32, kind="ExternalInput").ap()
    wo_d = nc.dram_tensor("wo", [D, D], F32, kind="ExternalInput").ap()
    wqi_d = nc.dram_tensor("wqi", [D, D], I32, kind="ExternalInput").ap()
    scales_d = nc.dram_tensor("scales", [D, D // 32], F32, kind="ExternalInput").ap()
    ident_d = nc.dram_tensor("ident", [128, 128], F32, kind="ExternalInput").ap()
    dmask_d = nc.dram_tensor("dmask", [128, 128], F32, kind="ExternalInput").ap()
    fcols_d = nc.dram_tensor("fcols", [128, 8 * NPAIR], F32, kind="ExternalInput").ap()
    fwide_d = nc.dram_tensor("fwide", [128, (NH - 8) * NWF], F32, kind="ExternalInput").ap()
    mwide_d = nc.dram_tensor("mwide", [128, 8 * 512], F32, kind="ExternalInput").ap()

    out_d = nc.dram_tensor("out", [1024, D], F32, kind="ExternalOutput").ap()
    scr_d = nc.dram_tensor("scr_awo", [1024, D], BF16, kind="Internal").ap()

    dbg = {}
    if debug:
        for nm, shape in [("dbg_xnt", [128, 1024]), ("dbg_kt", [128, 2048]),
                          ("dbg_vt", [128, NT * 129]), ("dbg_qt", [128, 2048]),
                          ("dbg_at", [128, 2048])]:
            dbg[nm] = nc.dram_tensor(nm, shape, F32, kind="ExternalOutput").ap()

    with ExitStack() as stk:
        en = stk.enter_context
        sems = {k: en(nc.semaphore(k)) for k in (
            "ld0", "ld1", "ld2", "ldm", "lax0", "lax1", "lax2", "lax3", "wtc", "wp0", "wp1",
            "wr0", "wr1", "wr2", "wr3", "scr0", "scr1", "scr2", "scr3", "so0", "so1", "so2", "so3", "dbg",
            "pe", "ve", "se", "ge")}
        E = Emitter(nc, sems)

        xnt = en(nc.sbuf_tensor("xnt", [128, 32 * 1024], BF16))
        kt = en(nc.sbuf_tensor("kt", [128, NT * 128], BF16))
        vt = en(nc.sbuf_tensor("vt", [128, NT * 129], BF16))
        ident = en(nc.sbuf_tensor("ident_s", [128, 128], BF16))
        dmask = en(nc.sbuf_tensor("dmask_s", [128, 128], BF16))

        stA = ExitStack()
        enA = stA.enter_context
        xblk = enA(nc.sbuf_tensor("xblk", [128, 3 * D], F32))
        xn_tm = enA(nc.sbuf_tensor("xn_tm", [128, 2 * D], BF16))
        xnt_o = enA(nc.sbuf_tensor("xnt_o", [128, 4 * D], BF16))
        wkvb = enA(nc.sbuf_tensor("wkvb", [128, 32 * 256], BF16))
        lnst = enA(nc.sbuf_tensor("lnst", [128, 2 * 48], F32))
        lnag = enA(nc.sbuf_tensor("lnag", [128, 2 * 2], F32))
        lnmv = enA(nc.sbuf_tensor("lnmv", [128, 2 * 4], F32))
        psA_tr = enA(nc.psum_tensor("psA_tr", [128, 2048], BF16))
        psA_k = enA(nc.psum_tensor("psA_k", [128, 512], F32))
        psA_v = enA(nc.psum_tensor("psA_v", [128, 512], F32))

        E.phase = "A"
        ge_ones = E.op("gpsimd", lambda e: e.memset(
            vt[:].rearrange("p (t c) -> p t c", c=129)[:, :, 128:129], 1.0), inc="ge")

        E.op("gpsimd", lambda e: e.dma_start(out=ident[:], in_=ident_d), inc="wtc")
        E.op("gpsimd", lambda e: e.dma_start(out=dmask[:], in_=dmask_d), inc="wtc")
        wt_consts = E.op("gpsimd", lambda e: e.dma_start(
            out=wkvb[:].rearrange("p (t c) -> p t c", c=256),
            in_=wkv_d.rearrange("(t p) c -> p t c", p=128)), inc="wtc")

        ve_xn = [0] * NT
        pe_tpr = [[0] * 4 for _ in range(NT)]
        veA_evr = [[0] * 4 for _ in range(NT)]
        seA_evr = [[0] * 4 for _ in range(NT)]
        pe_v = [0] * NT
        se_evv = [0] * NT
        pe_k = [0] * 4
        se_evk = [0] * 4
        ld_x = [0] * NT

        for t in range(NT):
            b3 = t % 3
            b2 = t % 2
            ld_x[t] = E.op("sync", lambda e, b3=b3, t=t: e.dma_start(
                out=xblk[:, b3 * D:(b3 + 1) * D], in_=x_d[t * 128:(t + 1) * 128, :]),
                wait=[("ve", ve_xn[t - 3] if t >= 3 else 0)], inc=f"ld{t % 3}")

            c1 = E.reserve("ve")
            c2 = E.reserve("ve")
            def lnstats(e, b2=b2, b3=b3, c1=c1, c2=c2):
                for c in range(8):
                    ins = e.bn_stats(lnst[:, b2 * 48 + c * 6: b2 * 48 + (c + 1) * 6],
                                     xblk[:, b3 * D + c * 512: b3 * D + (c + 1) * 512])
                ins.then_inc(sems["ve"], 1)
                e.wait_ge(sems["ve"], c1)
                e.bn_aggr(lnag[:, b2 * 2:(b2 + 1) * 2], lnst[:, b2 * 48:(b2 + 1) * 48]).then_inc(sems["ve"], 1)
                e.wait_ge(sems["ve"], c2)
                e.tensor_copy(lnmv[:, b2 * 4: b2 * 4 + 1], lnag[:, b2 * 2: b2 * 2 + 1])
                return e.tensor_scalar(lnmv[:, b2 * 4 + 1: b2 * 4 + 2],
                                       lnag[:, b2 * 2 + 1: b2 * 2 + 2], 1e-5, None, ALU.add)
            ve_stats = E.op("vector", lnstats,
                wait=[(f"ld{t % 3}", ld_x[t]), ("ve", ve_xn[t - 2] if t >= 2 else 0)], inc="ve")
            se_sq = E.op("scalar", lambda e, b2=b2: e.activation(
                lnmv[:, b2 * 4 + 2: b2 * 4 + 3], lnmv[:, b2 * 4 + 1: b2 * 4 + 2], AF.Sqrt),
                wait=[("ve", ve_stats)], inc="se")
            c3 = E.reserve("ve")
            def lnapply(e, b2=b2, b3=b3, c3=c3):
                e.reciprocal(lnmv[:, b2 * 4 + 3: b2 * 4 + 4],
                             lnmv[:, b2 * 4 + 2: b2 * 4 + 3]).then_inc(sems["ve"], 1)
                e.wait_ge(sems["ve"], c3)
                return e.tensor_scalar(xn_tm[:, b2 * D:(b2 + 1) * D], xblk[:, b3 * D:(b3 + 1) * D],
                                       lnmv[:, b2 * 4: b2 * 4 + 1], lnmv[:, b2 * 4 + 3: b2 * 4 + 4],
                                       ALU.subtract, ALU.mult)
            ve_xn[t] = E.op("vector", lnapply,
                wait=[("se", se_sq), ("pe", pe_tpr[t - 2][3] if t >= 2 else 0)], inc="ve")

            for g in range(4):
                gid = t * 4 + g
                ph_ = gid % 2
                if gid >= 2:
                    pg, pgg = (gid - 2) // 4, (gid - 2) % 4
                    prev_ev = seA_evr[pg][pgg]
                    prev_sem = "se" if (gid - 2) % 2 == 0 else "ve"
                else:
                    prev_ev, prev_sem = 0, "se"
                def tpr(e, b2=b2, g=g, ph_=ph_):
                    ins = None
                    for i in range(8):
                        dt = g * 8 + i
                        ins = e.transpose(psA_tr[:, ph_ * 1024 + i * 128: ph_ * 1024 + (i + 1) * 128],
                                          xn_tm[:, b2 * D + dt * 128: b2 * D + (dt + 1) * 128],
                                          ident[:])
                    return ins
                waits = [("ve", ve_xn[t]), (prev_sem, prev_ev)]
                if t == 0 and g == 0:
                    waits.append(("wtc", 48))
                pe_tpr[t][g] = E.op("tensor", tpr, wait=waits, inc="pe")
                if t < 8:
                    def evr(e, t=t, g=g, ph_=ph_):
                        dst = bass.AP(xnt, (g * 8) * 1024 + t * 128, [[32 * 1024, 128], [1024, 8], [1, 128]])
                        src_ = psA_tr[:, ph_ * 1024:(ph_ + 1) * 1024].rearrange("p (i c) -> p i c", c=128)
                        return e.copy(dst, src_) if hasattr(e, "copy") else e.tensor_copy(dst, src_)
                else:
                    def evr(e, t=t, g=g, ph_=ph_):
                        dst = bass.AP(xnt_o, (g * 8) * 512 + (t % 4) * 128, [[4 * D, 128], [512, 8], [1, 128]])
                        src_ = psA_tr[:, ph_ * 1024:(ph_ + 1) * 1024].rearrange("p (i c) -> p i c", c=128)
                        return e.copy(dst, src_) if hasattr(e, "copy") else e.tensor_copy(dst, src_)
                eng = "scalar" if gid % 2 == 0 else "vector"
                seA_evr[t][g] = E.op(eng, evr, wait=[("pe", pe_tpr[t][g])],
                                     inc="se" if gid % 2 == 0 else "ve")

            def vmm(e, t=t):
                ins = None
                for dt in range(32):
                    if t < 8:
                        lhsT = xnt[:, dt * 1024 + t * 128: dt * 1024 + (t + 1) * 128]
                    else:
                        lhsT = xnt_o[:, dt * 512 + (t % 4) * 128: dt * 512 + (t % 4 + 1) * 128]
                    ins = e.matmul(psA_v[:, 0:128], lhsT, wkvb[:, dt * 256 + 128: (dt + 1) * 256],
                                   start=(dt == 0), stop=(dt == 31))
                return ins
            pe_v[t] = E.op("tensor", vmm,
                           wait=[("se" if (t * 4 + 3) % 2 == 0 else "ve", seA_evr[t][3]),
                                 ("se" if (t * 4 + 2) % 2 == 0 else "ve", seA_evr[t][2]),
                                 ("wtc", 48),
                                 ("se", se_evv[t - 1] if t >= 1 else 0)], inc="pe")
            se_evv[t] = E.op("scalar", lambda e, t=t: e.copy(vt[:, t * 129: t * 129 + 128], psA_v[:, 0:128]),
                             wait=[("pe", pe_v[t])], inc="se")

            if t % 4 == 3:
                c = t // 4
                def kmm(e, c=c):
                    ins = None
                    for dt in range(32):
                        if c < 2:
                            rhs = xnt[:, dt * 1024 + c * 512: dt * 1024 + (c + 1) * 512]
                        else:
                            rhs = xnt_o[:, dt * 512: (dt + 1) * 512]
                        ins = e.matmul(psA_k[:, 0:512], wkvb[:, dt * 256: dt * 256 + 128], rhs,
                                       start=(dt == 0), stop=(dt == 31))
                    return ins
                pe_k[c] = E.op("tensor", kmm,
                               wait=[("se", se_evk[c - 1] if c >= 1 else 0)], inc="pe")
                se_evk[c] = E.op("scalar", lambda e, c=c: e.copy(kt[:, c * 512:(c + 1) * 512], psA_k[:, 0:512]),
                                 wait=[("pe", pe_k[c])], inc="se")

        A_ve, A_se, A_pe = E.cnt["ve"], E.cnt["se"], E.cnt["pe"]
        A_DONE = [("pe", A_pe), ("ve", A_ve), ("se", A_se)]
        stA.close()

        if debug:
            E.op("gpsimd", lambda e: e.dma_start(out=dbg["dbg_kt"], in_=kt[:]),
                 wait=[("se", A_se)], inc="dbg")
            E.op("gpsimd", lambda e: e.dma_start(out=dbg["dbg_vt"], in_=vt[:]),
                 wait=[("se", A_se), ("ge", ge_ones)], inc="dbg")
            E.op("gpsimd", lambda e: e.dma_start(
                out=dbg["dbg_xnt"], in_=xnt[:, 0 * 1024: 1 * 1024]),
                wait=[("se", A_se)], inc="dbg")

        stB = ExitStack()
        enB = stB.enter_context
        qat = enB(nc.sbuf_tensor("qat", [128, NH * 1024], BF16))
        wpan = enB(nc.sbuf_tensor("wpan", [128, 2 * 32 * 256], BF16))
        fcols = enB(nc.sbuf_tensor("fcols_s", [128, 8 * NPAIR], F32))
        fwide = enB(nc.sbuf_tensor("fwide_s", [128, (NH - 8) * NWF], F32))
        mwide = enB(nc.sbuf_tensor("mwide_s", [128, 8 * 512], BF16))
        expb = enB(nc.sbuf_tensor("expb", [128, 3072], BF16))
        attn_tm = enB(nc.sbuf_tensor("attn_tm", [128, 2 * 512], BF16))
        zz = enB(nc.sbuf_tensor("zz", [128, 16], F32))
        attn_un = enB(nc.sbuf_tensor("attn_un", [128, 1024], BF16))
        vf = enB(nc.sbuf_tensor("vf", [128, 2 * 16 * 129], BF16))
        awo = enB(nc.sbuf_tensor("awo", [128, 4 * 256], BF16))

        ld_fc = E.op("sync", lambda e: e.dma_start(out=fcols[:], in_=fcols_d), wait=A_DONE, inc="ldm")
        E.op("sync", lambda e: e.dma_start(out=fwide[:], in_=fwide_d), inc="ldm")
        wt_mw = E.op("gpsimd", lambda e: e.dma_start(out=mwide[:], in_=mwide_d), wait=A_DONE, inc="wtc")
        ld_fc = 32

        stPB = ExitStack()
        psB = stPB.enter_context(nc.psum_tensor("psB", [128, 2048], F32))

        E.phase = "B"
        wt_pan = [0] * 16
        pe_pang = {}
        se_evq = {}
        group_i = 0
        for p in range(16):
            bufp = p % 2
            warw = [("pe", pe_pang[(p - 2, 3)])] if p >= 2 else list(A_DONE)
            wt_pan[p] = E.op("gpsimd", lambda e, p=p, bufp=bufp: e.dma_start(
                out=wpan[:, bufp * 8192: (bufp + 1) * 8192].rearrange("p (t c) -> p t c", c=256),
                in_=wq_d[:, p * 256:(p + 1) * 256].rearrange("(t q) c -> q t c", q=128)),
                wait=warw, inc=f"wp{p % 2}")
            for sub in range(2):
                for qc in range(2):
                    g = sub * 2 + qc
                    quarter = group_i % 4
                    group_i += 1
                    def bmm(e, bufp=bufp, sub=sub, qc=qc, quarter=quarter):
                        ins = None
                        for dt in range(32):
                            ins = e.matmul(
                                psB[:, quarter * 512:(quarter + 1) * 512],
                                wpan[:, bufp * 8192 + dt * 256 + sub * 128: bufp * 8192 + dt * 256 + (sub + 1) * 128],
                                xnt[:, dt * 1024 + qc * 512: dt * 1024 + (qc + 1) * 512],
                                start=(dt == 0), stop=(dt == 31))
                        return ins
                    waits = [(f"wp{p % 2}", wt_pan[p]), ("se", se_evq.get(group_i - 1 - 4, 0))]
                    if p == 0 and g == 0:
                        waits.append(("se", A_se))
                        waits.append(("pe", A_pe))
                    pe_pang[(p, g)] = E.op("tensor", bmm, wait=waits, inc="pe")
                    h = p * 2 + sub
                    se_evq[group_i - 1] = E.op("scalar", lambda e, h=h, qc=qc, quarter=quarter: e.mul(
                        qat[:, h * 1024 + qc * 512: h * 1024 + (qc + 1) * 512],
                        psB[:, quarter * 512:(quarter + 1) * 512], ATT_SCALE),
                        wait=[("pe", pe_pang[(p, g)])], inc="se")
        B_se = E.cnt["se"]

        dbg_qt_cnt = 0
        if debug:
            dbg_qt_cnt = E.op("gpsimd", lambda e: e.dma_start(out=dbg["dbg_qt"], in_=qat[:, 0:2048]),
                              wait=[("se", B_se)], inc="dbg")

        E.phase = "C"
        stPB.close()
        stPC = ExitStack()
        psC_sc = stPC.enter_context(nc.psum_tensor("psC_sc", [128, 2048], F32))
        psC_av = stPC.enter_context(nc.psum_tensor("psC_av", [128, 1024], F32))
        psC_tr = stPC.enter_context(nc.psum_tensor("psC_tr", [128, 1024], BF16))

        AVOFF = [0, 129, 512, 641]
        se_exp = {}
        ve_f = {}
        ge_f = {}
        pe_avg = {}
        se_evA = {}
        ve_evB = {}
        pe_ctp = {}
        se_evat = {}
        pending = None

        def _flush_tail(p):
            i = p["idx"]
            zr_p = E.reserve("ve")
            fn_p = p["evB"]
            ve_evB[i] = E.op("vector", lambda e, fn_p=fn_p, zr_p=zr_p: fn_p(e, zr=zr_p),
                             wait=[("ve", se_evA[i]), ("pe", pe_ctp.get(i - 2, 0))], inc="ve")
            pe_ctp[i] = E.op("tensor", p["ctp"],
                             wait=[("ve", ve_evB[i]), ("ve", se_evat.get(i - 2, 0))], inc="pe")
            w = [("pe", pe_ctp[i])]
            if p["dbg"]:
                w.append(("dbg", dbg_qt_cnt))
            se_evat[i] = E.op("vector", p["evat"], wait=w, inc="ve")

        it_c = 0
        KP = _keepers()
        last_av = {}
        fst = {"v": True, "g": True}
        for r in range(2):
            for h in range(NH):
                if h >= WIDE_H0:
                    kbs = [kb for kb in _SC_KBS[r] if KP["kw"][(h, r, kb)]]
                else:
                    kbs = [kb for kb in _SC_KBS[r]
                           if any(kb in set(_Lj(4 * r + jj)) and KP["kn"][(h, 4 * r + jj, kb)]
                                  for jj in range(4))]
                groups = [kbs[i:i + 2] for i in range(0, len(kbs), 2)]
                G = len(groups)
                emits = []
                for gi2, grp2 in enumerate(groups):
                    for i2, kb2 in enumerate(grp2):
                        for jj2 in range(4):
                            j2 = 4 * r + jj2
                            if kb2 not in set(_Lj(j2)):
                                continue
                            if h < WIDE_H0 and not KP["kn"][(h, j2, kb2)]:
                                continue
                            emits.append((gi2, i2, kb2, jj2))
                bank_first = {}
                bank_last = {}
                for em in emits:
                    b = em[3] // 2
                    bank_first.setdefault(b, em)
                    bank_last[b] = em
                assert set(bank_first) == {0, 1}, (r, h, kbs)
                emits = set(emits)
                idx0 = r * NH + h
                bufv = idx0 % 2
                ve_vf = 0
                if h >= WIDE_H0:
                    def vfb(e, h=h, r=r, kbs=kbs, bufv=bufv):
                        ins = None
                        for ki, kb in enumerate(kbs):
                            ci = ((h - WIDE_H0) * NWF
                                  + (0 if r == 0 else len(WKB[0])) + WKB[r].index(kb))
                            ins = e.tensor_scalar(
                                vf[:, bufv * 2064 + ki * 129: bufv * 2064 + (ki + 1) * 129],
                                vt[:, kb * 129: (kb + 1) * 129],
                                fwide[:, ci: ci + 1], None, ALU.mult)
                        return ins
                    vfw = [("ldm", 32), ("ge", ge_ones)]
                    if idx0 >= 2:
                        vfw.append(("pe", last_av[idx0 - 2]))
                    ve_vf = E.op("vector", vfb, wait=vfw, inc="ve")
                for gi in range(G + 2):
                    if gi < G:
                        grp = groups[gi]
                        slot = gi % 2
                        it = it_c + gi
                        eslot = it % 3
                        def smm(e, grp=grp, slot=slot, h=h, r=r):
                            ins = None
                            for i, kb in enumerate(grp):
                                ins = e.matmul(psC_sc[:, slot * 1024 + i * 512: slot * 1024 + (i + 1) * 512],
                                               kt[:, kb * 128:(kb + 1) * 128],
                                               qat[:, h * 1024 + r * 512: h * 1024 + (r + 1) * 512],
                                               start=True, stop=True)
                            return ins
                        waits = [("se", se_exp.get(it - 2, 0))]
                        if it == 0:
                            waits += [("se", B_se)]
                        sc_pe = E.op("tensor", smm, wait=waits, inc="pe")
                        if gi == 0 and pending is not None:
                            i_p = pending["idx"]
                            zr_p = E.reserve("ve")
                            fn_p = pending["evB"]
                            ve_evB[i_p] = E.op("vector",
                                               lambda e, fn_p=fn_p, zr_p=zr_p: fn_p(e, zr=zr_p),
                                               wait=[("ve", se_evA[i_p]),
                                                     ("pe", pe_ctp.get(i_p - 2, 0))], inc="ve")
                        if gi == min(3, G - 1) and pending is not None:
                            i_p = pending["idx"]
                            pe_ctp[i_p] = E.op("tensor", pending["ctp"],
                                               wait=[("ve", ve_evB[i_p]),
                                                     ("ve", se_evat.get(i_p - 2, 0))], inc="pe")
                            w = [("pe", pe_ctp[i_p])]
                            if pending["dbg"]:
                                w.append(("dbg", dbg_qt_cnt))
                            se_evat[i_p] = E.op("vector", pending["evat"], wait=w, inc="ve")
                            pending = None
                        se_exp[it] = E.op("scalar", lambda e, slot=slot, eslot=eslot, n=len(grp): e.activation(
                            expb[:, eslot * 1024: eslot * 1024 + n * 512],
                            psC_sc[:, slot * 1024: slot * 1024 + n * 512], AF.Exp),
                            wait=[("pe", sc_pe), ("pe", pe_avg.get(it - 3, 0))], inc="se")
                        def fm(e, grp=grp, slot=eslot, h=h, r=r, jjs=(0, 1, 2, 3)):
                            ins = None
                            for i, kb in enumerate(grp):
                                if h >= WIDE_H0:
                                    sl = expb[:, slot * 1024 + i * 512: slot * 1024 + (i + 1) * 512]
                                    if kb in BAND[r]:
                                        t = r * 4 + BAND[r].index(kb)
                                        ins = e.tensor_tensor(sl, sl,
                                                              mwide[:, t * 512:(t + 1) * 512],
                                                              ALU.mult)
                                    continue
                                for jj in jjs:
                                    j = 4 * r + jj
                                    if kb not in set(_Lj(j)) or not _KEEP_CACHE["kn"][(h, j, kb)]:
                                        continue
                                    col = slot * 1024 + i * 512 + jj * 128
                                    fc = fcols[:, h * NPAIR + _PAIR_IDX[(j, kb)]: h * NPAIR + _PAIR_IDX[(j, kb)] + 1]
                                    sl = expb[:, col: col + 128]
                                    if kb == j:
                                        ins = e.scalar_tensor_tensor(sl, sl, fc, dmask[:], ALU.mult, ALU.mult)
                                    else:
                                        ins = e.tensor_scalar(sl, sl, fc, None, ALU.mult)
                            return ins
                        fmw = [("se", se_exp[it])]
                        if fst["v"]:
                            fmw += [("ldm", ld_fc), ("wtc", wt_mw)]
                        if h >= WIDE_H0:
                            if any(kb in BAND[r] for kb in grp):
                                ve_f[it] = E.op("vector", fm, wait=fmw, inc="ve")
                                if fst["v"]:
                                    fst["v"] = False
                            else:
                                ve_f[it] = None
                            ge_f[it] = None
                        elif h < WIDE_H0:
                            gp_pairs = [(i, kb) for i, kb in enumerate(grp)
                                        if kb in set(_Lj(4 * r + 3)) and kb != 4 * r + 3
                                        and _KEEP_CACHE["kn"][(h, 4 * r + 3, kb)]]
                            vdiag = [(i, kb) for i, kb in enumerate(grp) if kb == 4 * r + 3]
                            def fmv(e, grp=grp, slot=eslot, h=h, r=r, vdiag=vdiag):
                                ins = fm(e, grp=grp, slot=slot, h=h, r=r, jjs=(0, 1, 2))
                                for i, kb in vdiag:
                                    j = 4 * r + 3
                                    if not _KEEP_CACHE["kn"][(h, j, kb)]:
                                        continue
                                    col = slot * 1024 + i * 512 + 3 * 128
                                    fc = fcols[:, h * NPAIR + _PAIR_IDX[(j, kb)]: h * NPAIR + _PAIR_IDX[(j, kb)] + 1]
                                    sl = expb[:, col: col + 128]
                                    ins = e.scalar_tensor_tensor(sl, sl, fc, dmask[:], ALU.mult, ALU.mult)
                                return ins
                            def fmg(e, grp=grp, slot=eslot, h=h, r=r, gp_pairs=gp_pairs):
                                ins = None
                                for i, kb in enumerate(grp):
                                    if (i, kb) not in gp_pairs:
                                        continue
                                    j = 4 * r + 3
                                    col = slot * 1024 + i * 512 + 3 * 128
                                    ci = h * NPAIR + _PAIR_IDX[(j, kb)]
                                    sl = expb[:, col: col + 128]
                                    ins = e.tensor_tensor(sl, sl,
                                                          fcols[:, ci: ci + 1].to_broadcast([128, 128]),
                                                          ALU.mult)
                                return ins
                            nv = sum(1 for i, kb in enumerate(grp) for jj in (0, 1, 2)
                                     if kb in set(_Lj(4 * r + jj)) and KP["kn"][(h, 4 * r + jj, kb)])
                            nv += sum(1 for i, kb in vdiag if KP["kn"][(h, 4 * r + 3, kb)])
                            gmw = [("se", se_exp[it])]
                            if fst["g"]:
                                gmw += [("ldm", ld_fc), ("wtc", wt_mw)]
                            if nv:
                                ve_f[it] = E.op("vector", fmv, wait=fmw, inc="ve")
                                if fst["v"]:
                                    fst["v"] = False
                            else:
                                ve_f[it] = None
                            if gp_pairs:
                                ge_f[it] = E.op("gpsimd", fmg, wait=gmw, inc="ge")
                                fst["g"] = False
                            else:
                                ge_f[it] = None

                    if gi >= 2:
                        grp = groups[gi - 2]
                        it = it_c + gi - 2
                        eslot = it % 3
                        def avm(e, grp=grp, gidx=gi - 2, slot=eslot, r=r, h=h, kbs=kbs, bufv=bufv,
                                emits=emits, bank_first=bank_first, bank_last=bank_last):
                            ins = None
                            for i, kb in enumerate(grp):
                                if h >= WIDE_H0:
                                    rhs = vf[:, bufv * 2064 + kbs.index(kb) * 129:
                                             bufv * 2064 + (kbs.index(kb) + 1) * 129]
                                else:
                                    rhs = vt[:, kb * 129: (kb + 1) * 129]
                                for jj in range(4):
                                    key = (gidx, i, kb, jj)
                                    if key not in emits:
                                        continue
                                    b = jj // 2
                                    ins = e.matmul(psC_av[:, AVOFF[jj]: AVOFF[jj] + 129],
                                                   expb[:, slot * 1024 + i * 512 + jj * 128: slot * 1024 + i * 512 + (jj + 1) * 128],
                                                   rhs,
                                                   start=(bank_first[b] == key),
                                                   stop=(bank_last[b] == key))
                            return ins
                        avw = []
                        if ve_f.get(it):
                            avw.append(("ve", ve_f[it]))
                        else:
                            avw.append(("se", se_exp[it]))
                        if ge_f.get(it):
                            avw.append(("ge", ge_f[it]))
                        if gi == 2 and h >= WIDE_H0:
                            avw.append(("ve", ve_vf))
                        if gi == 2:
                            avw.append(("ve", se_evA.get(r * NH + h - 1, 0)))
                            if it == 0:
                                avw.append(("ge", ge_ones))
                        pe_avg[it] = E.op("tensor", avm, wait=avw, inc="pe")
                it_c += G
                last_av[idx0] = pe_avg[it_c - 1]

                av_fin = pe_avg[it_c - 1]
                idx = r * NH + h
                bufa = idx % 2
                half = idx % 2
                def evA(e, bufa=bufa):
                    src4 = bass.AP(psC_av, 0, [[1024, 128], [512, 2], [129, 2], [1, 128]])
                    srcz = bass.AP(psC_av, 128, [[1024, 128], [512, 2], [129, 2], [1, 1]])
                    e.tensor_copy(zz[:, bufa * 8: bufa * 8 + 4].rearrange("p (a b y) -> p a b y", a=2, b=2),
                                  srcz)
                    return e.tensor_copy(
                        attn_un[:, bufa * 512: (bufa + 1) * 512].rearrange("p (a b y) -> p a b y", a=2, b=2),
                        src4)
                se_evA[idx] = E.op("vector", evA,
                                   wait=[("pe", av_fin), ("ve", ve_evB.get(idx - 2, 0))], inc="ve")

                def evB(e, bufa=bufa, zr=None):
                    e.reciprocal(zz[:, bufa * 8 + 4: bufa * 8 + 8],
                                 zz[:, bufa * 8: bufa * 8 + 4]).then_inc(sems["ve"], 1)
                    e.wait_ge(sems["ve"], zr)
                    srcu = attn_un[:, bufa * 512: (bufa + 1) * 512].rearrange(
                        "p (a b y) -> p a b y", a=2, b=2)
                    dst = attn_tm[:, bufa * 512: (bufa + 1) * 512].rearrange(
                        "p (a b y) -> p a b y", a=2, b=2)
                    return e.tensor_tensor(dst, srcu,
                                           zz[:, bufa * 8 + 4: bufa * 8 + 8]
                                           .rearrange("p (a b) -> p a b", a=2)
                                           .to_broadcast([128, 2, 2, 128]), ALU.mult)
                def ctp(e, bufa=bufa, half=half):
                    ins = None
                    for jj in range(4):
                        ins = e.transpose(psC_tr[:, half * 512 + jj * 128: half * 512 + (jj + 1) * 128],
                                          attn_tm[:, bufa * 512 + jj * 128: bufa * 512 + (jj + 1) * 128],
                                          ident[:])
                    return ins
                def evat(e, half=half, h=h, r=r):
                    src_ = psC_tr[:, half * 512:(half + 1) * 512]
                    dst_ = qat[:, h * 1024 + r * 512: h * 1024 + (r + 1) * 512]
                    return e.copy(dst_, src_) if hasattr(e, "copy") else e.tensor_copy(dst_, src_)
                pending = {"evB": evB, "ctp": ctp, "evat": evat, "idx": idx,
                           "dbg": bool(debug and r == 0 and h < 2)}


        if pending is not None:
            _flush_tail(pending)
            pending = None
        C_se = E.cnt["se"]
        C_ve = E.cnt["ve"]

        if debug:
            E.op("gpsimd", lambda e: e.dma_start(out=dbg["dbg_at"], in_=qat[:, 0:2048]),
                 wait=[("se", C_se), ("ve", C_ve)], inc="dbg")

        E.phase = "D"
        stPC.close()
        stPD = ExitStack()
        psB = stPD.enter_context(nc.psum_tensor("psD", [128, 2048], F32))
        it_d = 0
        pe_dmm = {}
        ve_evd = {}
        st_awo = {}
        wt_dpan = [0] * 16
        for p in range(16):
            bufp = p % 2
            war = pe_dmm[(p - 2, 7)] if p >= 2 else pe_pang[(14 + p, 3)]
            wt_dpan[p] = E.op("gpsimd", lambda e, p=p, bufp=bufp: e.dma_start(
                out=wpan[:, bufp * 8192: (bufp + 1) * 8192].rearrange("p (t c) -> p t c", c=256),
                in_=wo_d[:, p * 256:(p + 1) * 256].rearrange("(t q) c -> q t c", q=128)),
                wait=[("pe", war)], inc=f"wp{p % 2}")
            for qb in range(8):
                slot = it_d % 4
                def dmm(e, bufp=bufp, qb=qb, slot=slot):
                    ins = None
                    for h in range(NH):
                        ins = e.matmul(psB[:, slot * 512: slot * 512 + 256],
                                       qat[:, h * 1024 + qb * 128: h * 1024 + (qb + 1) * 128],
                                       wpan[:, bufp * 8192 + h * 256: bufp * 8192 + (h + 1) * 256],
                                       start=(h == 0), stop=(h == NH - 1))
                    return ins
                waits = [(f"wp{p % 2}", wt_dpan[p]), ("ve", ve_evd.get(it_d - 4, 0))]
                if it_d == 0:
                    waits.append(("se", C_se))
                    waits.append(("ve", C_ve))
                pe_dmm[(p, qb)] = E.op("tensor", dmm, wait=waits, inc="pe")
                def dev(e, slot=slot, it_d=it_d):
                    return e.tensor_copy(awo[:, slot * 256:(slot + 1) * 256],
                                         psB[:, slot * 512: slot * 512 + 256])
                ve_evd[it_d] = E.op("vector", dev,
                                    wait=[("pe", pe_dmm[(p, qb)]),
                                          (f"scr{it_d % 4}", st_awo.get(it_d - 4, 0))], inc="ve")
                st_awo[it_d] = E.op("sync", lambda e, p=p, qb=qb, slot=slot: e.dma_start(
                    out=scr_d[qb * 128:(qb + 1) * 128, p * 256:(p + 1) * 256],
                    in_=awo[:, slot * 256:(slot + 1) * 256]),
                    wait=[("ve", ve_evd[it_d])], inc=f"scr{it_d % 4}")
                it_d += 1
        D_pe = E.cnt["pe"]
        D_scr = [E.cnt[f"scr{i}"] for i in range(4)]
        D_DONE = ([("pe", D_pe), ("ve", E.cnt["ve"]), ("se", E.cnt["se"])]
                  + [(f"scr{i}", D_scr[i]) for i in range(4)])
        if debug:
            D_DONE.append(("dbg", E.cnt["dbg"]))
        stPD.close()
        stB.close()

        E.phase = "EE"
        stE = ExitStack()
        enE = stE.enter_context
        wqr = enE(nc.sbuf_tensor("wqr", [128, 4 * D], BF16))
        scalesb = enE(nc.sbuf_tensor("scalesb", [128, 32 * 128], F32))
        deq = enE(nc.sbuf_tensor("deq", [128, 4 * D], BF16))
        wtp = enE(nc.sbuf_tensor("wtp", [128, 2 * 32 * 256], BF16))
        awin = enE(nc.sbuf_tensor("awin", [128, 4 * 256], BF16))
        xin = enE(nc.sbuf_tensor("xin", [128, 4 * 256], F32))
        outst = enE(nc.sbuf_tensor("outst", [128, 4 * 256], F32))
        psE_tr = enE(nc.psum_tensor("psE_tr", [128, 2048], BF16))
        psE = enE(nc.psum_tensor("psE", [128, 2048], F32))

        ld_sc = E.op("sync", lambda e: e.dma_start(
            out=scalesb[:].rearrange("p (t c) -> p t c", c=128),
            in_=scales_d.rearrange("(t p) c -> p t c", p=128)), wait=D_DONE, inc="ldm")

        wt_wqr = [0] * 32
        ve_deq = [0] * 32
        pe_etp = [[0] * 4 for _ in range(32)]
        se_evw = [[0] * 4 for _ in range(32)]
        pe_emm = {}
        ve_cmb = {}
        st_out = {}
        ld_ax = {}
        it_e = 0

        def _e_prod_head(ob):
            b4 = ob % 4
            wqrw = ([("ve", ve_deq[ob - 4])] if ob >= 4 else list(D_DONE))
            wt_wqr[ob] = E.op("gpsimd", lambda e, ob=ob, b4=b4: e.dma_start(
                out=wqr[:, b4 * D:(b4 + 1) * D], in_=wqi_d[ob * 128:(ob + 1) * 128, :]),
                wait=wqrw, inc=f"wr{ob % 4}")
            def dq(e, ob=ob, b4=b4):
                return e.scalar_tensor_tensor(
                    deq[:, b4 * D:(b4 + 1) * D].rearrange("p (s c) -> p s c", c=32),
                    wqr[:, b4 * D:(b4 + 1) * D].rearrange("p (s c) -> p s c", c=32),
                    128.0,
                    scalesb[:, ob * 128:(ob + 1) * 128].to_broadcast([128, 128, 32]),
                    ALU.subtract, ALU.mult)
            dqw = [(f"wr{ob % 4}", wt_wqr[ob]), ("pe", pe_etp[ob - 4][3] if ob >= 4 else 0)]
            if ob == 0:
                dqw.append(("ldm", ld_sc))
            ve_deq[ob] = E.op("vector", dq, wait=dqw, inc="ve")

        def _e_prod_round(ob, g):
            b2 = ob % 4
            bufp = (ob // 2) % 2
            gid = ob * 4 + g
            ph = gid % 2
            if gid >= 2:
                pg, pgg = (gid - 2) // 4, (gid - 2) % 4
                prev = se_evw[pg][pgg]
                prev_sem = "se"
            else:
                prev, prev_sem = 0, "se"
            def etp(e, b2=b2, g=g, ph=ph):
                ins = None
                for i in range(8):
                    it = g * 8 + i
                    ins = e.transpose(psE_tr[:, ph * 1024 + i * 128: ph * 1024 + (i + 1) * 128],
                                      deq[:, b2 * D + it * 128: b2 * D + (it + 1) * 128],
                                      ident[:])
                return ins
            pe_etp[ob][g] = E.op("tensor", etp,
                                 wait=[("ve", ve_deq[ob]), (prev_sem, prev)], inc="pe")
            def evw(e, ob=ob, g=g, bufp=bufp, ph=ph):
                dst = bass.AP(wtp, bufp * 8192 + (g * 8) * 256 + (ob % 2) * 128,
                              [[2 * 32 * 256, 128], [256, 8], [1, 128]])
                src_ = psE_tr[:, ph * 1024:(ph + 1) * 1024].rearrange("p (i c) -> p i c", c=128)
                return e.copy(dst, src_) if hasattr(e, "copy") else e.tensor_copy(dst, src_)
            warw = pe_emm.get((ob // 2 - 2, 7), 0)
            se_evw[ob][g] = E.op("scalar", evw, wait=[("pe", pe_etp[ob][g]), ("pe", warw)],
                                 inc="se")

        pf_state = {"pf": 0}

        def _e_prefetch(it2):
            pp2, qb2 = it2 // 8, it2 % 8
            bio = it2 % 4
            axw = [("ve", ve_cmb.get(it2 - 4, 0))]
            if it2 == 0:
                axw += [(f"scr{i}", D_scr[i]) for i in range(4)]
            E.op("sync", lambda e, pp=pp2, qb=qb2, bio=bio: e.dma_start(
                out=awin[:, bio * 256:(bio + 1) * 256],
                in_=scr_d[qb * 128:(qb + 1) * 128, pp * 256:(pp + 1) * 256]),
                wait=axw, inc=f"lax{bio}")
            ld_ax[it2] = E.op("sync", lambda e, pp=pp2, qb=qb2, bio=bio: e.dma_start(
                out=xin[:, bio * 256:(bio + 1) * 256],
                in_=x_d[qb * 128:(qb + 1) * 128, pp * 256:(pp + 1) * 256]), inc=f"lax{bio}")

        def _e_consume(pp, qb):
            nonlocal it_e
            bufp = pp % 2
            slot = it_e % 4
            bio = it_e % 4
            while pf_state["pf"] <= min(it_e + 2, 127):
                _e_prefetch(pf_state["pf"])
                pf_state["pf"] += 1
            def emm(e, pp=pp, qb=qb, slot=slot, bufp=bufp):
                ins = None
                for dt in range(32):
                    ins = e.matmul(psE[:, slot * 512: slot * 512 + 256],
                                   xnt[:, dt * 1024 + qb * 128: dt * 1024 + (qb + 1) * 128],
                                   wtp[:, bufp * 8192 + dt * 256: bufp * 8192 + (dt + 1) * 256],
                                   start=(dt == 0), stop=(dt == 31))
                return ins
            obs = (2 * pp, 2 * pp + 1)
            pw_se = max(se_evw[o][g2] for o in obs for g2 in range(4))
            pe_emm[(pp, qb)] = E.op("tensor", emm,
                                    wait=[("se", pw_se),
                                          ("ve", ve_cmb.get(it_e - 4, 0))], inc="pe")
            e1 = E.reserve("ve")
            def cmb(e, slot=slot, bio=bio, e1=e1):
                e.tensor_tensor(outst[:, bio * 256:(bio + 1) * 256],
                                psE[:, slot * 512: slot * 512 + 256],
                                awin[:, bio * 256:(bio + 1) * 256],
                                ALU.add).then_inc(sems["ve"], 1)
                e.wait_ge(sems["ve"], e1)
                return e.tensor_tensor(outst[:, bio * 256:(bio + 1) * 256],
                                       outst[:, bio * 256:(bio + 1) * 256],
                                       xin[:, bio * 256:(bio + 1) * 256], ALU.add)
            ve_cmb[it_e] = E.op("vector", cmb,
                                wait=[("pe", pe_emm[(pp, qb)]), (f"lax{bio}", ld_ax[it_e]),
                                      (f"so{bio}", st_out.get(it_e - 4, 0))], inc="ve")
            st_out[it_e] = E.op("sync", lambda e, pp=pp, qb=qb, bio=bio: e.dma_start(
                out=out_d[qb * 128:(qb + 1) * 128, pp * 256:(pp + 1) * 256],
                in_=outst[:, bio * 256:(bio + 1) * 256]),
                wait=[("ve", ve_cmb[it_e])], inc=f"so{bio}")
            it_e += 1

        for ob in range(4):
            _e_prod_head(ob)
        for wave in range(17):
            obs = [2 * wave, 2 * wave + 1] if wave < 16 else []
            rounds = [(ob, g) for ob in obs for g in range(4)]
            for k in range(8):
                if wave >= 1:
                    _e_consume(wave - 1, k)
                if rounds:
                    _e_prod_round(*rounds[k])
                if k == 7 and wave < 15:
                    for ob2 in (2 * wave + 4, 2 * wave + 5):
                        if ob2 <= 31:
                            _e_prod_head(ob2)
        stE.close()

        finw = [(f"so{i}", E.cnt[f"so{i}"]) for i in range(4)]
        if debug:
            finw.append(("dbg", E.cnt["dbg"]))
        E.op("sync", lambda e: None, wait=finw)

        with nc.Block() as block:
            @block.sync
            def _(eng):
                for f in E.ops["sync"]:
                    f(eng)

            @block.gpsimd
            def _(eng):
                for f in E.ops["gpsimd"]:
                    f(eng)

            @block.vector
            def _(eng):
                for f in E.ops["vector"]:
                    f(eng)

            @block.scalar
            def _(eng):
                for f in E.ops["scalar"]:
                    f(eng)

            @block.tensor
            def _(eng):
                for f in E.ops["tensor"]:
                    f(eng)

    return nc


def _host_inputs(half, x_b, wq, wkv, wo, wqi, scales):
    own = own_chrono_blocks(half)
    perm = perm_chrono_blocks(half)
    idx = np.concatenate([np.arange(t * 128, (t + 1) * 128) for t in perm])
    xp = np.ascontiguousarray(x_b[idx])
    return {
        "x": xp, "wq": wq, "wkv": wkv, "wo": wo, "wqi": wqi, "scales": scales,
        "ident": np.eye(128, dtype=np.float32),
        "dmask": np.triu(np.ones((128, 128), dtype=np.float32)),
        "fcols": build_fcols(half),
        "fwide": build_fwide(half),
        "mwide": build_maskwide(half),
    }


def kernel(L0_attn_hidden_states, L0_attn_Wq, L0_attn_Wkv, L0_attn_Wo,
           L0_ffn_w_q, L0_ffn_scales):
    x = np.ascontiguousarray(np.asarray(L0_attn_hidden_states, dtype=np.float32))
    wq = np.ascontiguousarray(np.asarray(L0_attn_Wq, dtype=np.float32))
    wkv = np.ascontiguousarray(np.asarray(L0_attn_Wkv, dtype=np.float32))
    wo = np.ascontiguousarray(np.asarray(L0_attn_Wo, dtype=np.float32))
    wqi = np.ascontiguousarray(np.asarray(L0_ffn_w_q, dtype=np.int32))
    scales = np.ascontiguousarray(np.asarray(L0_ffn_scales, dtype=np.float32))

    nc = build_nc()
    in_maps = []
    for c in range(8):
        b, half = c // 2, c % 2
        in_maps.append(_host_inputs(half, x[b], wq, wkv, wo, wqi, scales))

    res = run_bass_kernel_spmd(nc, in_maps, core_ids=list(range(8)))

    out = np.empty((B, S, D), dtype=np.float32)
    for c in range(8):
        b, half = c // 2, c % 2
        oc = np.asarray(res.results[c]["out"], dtype=np.float32)
        for i, t in enumerate(own_chrono_blocks(half)):
            out[b, t * 128:(t + 1) * 128, :] = oc[i * 128:(i + 1) * 128, :]
    return out
